# version 20
# baseline (speedup 1.0000x reference)
import numpy as np
from contextlib import ExitStack

import concourse.bass as bass
import concourse.mybir as mybir
from concourse.bass_utils import run_bass_kernel_spmd
from concourse.alu_op_type import AluOpType as OP

F32 = mybir.dt.float32
BF16 = mybir.dt.bfloat16
I32 = mybir.dt.int32
AF = mybir.ActivationFunctionType
AX = mybir.AxisListType.X

NCORES = 8
TOK = 2048
TC = 16
D = 4096
KC = 32
KH = 16
NTN = 8
NTW = 512
TG = 4
NWS = 3
EPS = 1e-5
CAPACITY = float(int(3.0 * (NCORES * TOK) / 8))
BIG_NEG = -1.0e30
ALPHA = 1.0 / 16.0


class Sem:

    def __init__(self, es, nc, name):
        self.h = es.enter_context(nc.semaphore(name))
        self.v = 0

    def inc(self, instr, k=1):
        instr.then_inc(self.h, k)
        self.v += k
        return instr


def ln_chain(V, A, x_ap, out_ap, st, mv, nm, nmr, rstd, eps_t, sv, sa, nstats,
             APL=None):
    APL = APL if APL is not None else V
    w = D // nstats
    for i in range(nstats):
        V.bn_stats(st[:, 6 * i:6 * i + 6], x_ap[:, w * i:w * (i + 1)])
    V.drain()
    V.bn_aggr(mv[:, :], st[:, 0:6 * nstats])
    V.drain()
    sv.inc(V.tensor_scalar(nm[:, :], mv[:, 0:1], -1.0, None, OP.mult))
    A.wait_ge(sv.h, sv.v)
    A.activation(rstd[:, :], mv[:, 1:2], AF.Ln, bias=eps_t[:, 0:1], scale=1.0)
    A.drain()
    sa.inc(A.activation(rstd[:, :], rstd[:, :], AF.Exp, scale=-0.5))
    APL.wait_ge(sa.h, sa.v)
    if APL is not V:
        APL.wait_ge(sv.h, sv.v)
    APL.tensor_tensor(nmr[:, :], nm[:, :], rstd[:, :], OP.mult)
    if APL is V:
        APL.drain()
    return APL.tensor_scalar(out_ap, x_ap, rstd[:, 0:1], nmr[:, 0:1], OP.mult, OP.add)



def build():
    nc = bass.Bass("TRN2", target_bir_lowering=False, debug=False)

    x_d = nc.dram_tensor("x", [TOK, D], F32, kind="ExternalInput").ap()
    waT_d = nc.dram_tensor("waT", [D, D], F32, kind="ExternalInput").ap()
    maskR_d = nc.dram_tensor("maskR", [128, 16, 1024], I32, kind="ExternalInput").ap()
    aRe_d = nc.dram_tensor("aRe", [D, 128], F32, kind="ExternalInput").ap()
    gw1T_d = nc.dram_tensor("gw1T", [D, 32], F32, kind="ExternalInput").ap()
    gw2T_d = nc.dram_tensor("gw2T", [128, 8], F32, kind="ExternalInput").ap()
    gb1B_d = nc.dram_tensor("gb1B", [128, 32], F32, kind="ExternalInput").ap()
    gamB_d = nc.dram_tensor("gamB", [128, 32], F32, kind="ExternalInput").ap()
    betB_d = nc.dram_tensor("betB", [128, 32], F32, kind="ExternalInput").ap()
    gb2B_d = nc.dram_tensor("gb2B", [128, 8], F32, kind="ExternalInput").ap()
    wfT_d = nc.dram_tensor("wfT", [D, D], F32, kind="ExternalInput").ap()
    bRe_d = nc.dram_tensor("bRe", [128, D], F32, kind="ExternalInput").ap()

    out_d = nc.dram_tensor("out", [TOK, D], F32, kind="ExternalOutput").ap()

    b_dram = nc.dram_tensor("b_scr", [TOK, D], BF16)
    bin_dram = nc.dram_tensor("bin_scr", [128, 128], BF16)
    x2_d = nc.dram_tensor("x2_scr", [TOK, D], F32).ap()
    l2t_d = nc.dram_tensor("l2t_scr", [TOK, D], BF16).ap()
    e_dram = nc.dram_tensor("e_scr", [TOK, D], BF16)
    cc_in = nc.dram_tensor("cc_in", [1, 16], F32)
    cc_out = nc.dram_tensor("cc_out", [1, 16], F32)
    awd_dram = nc.dram_tensor("awd_scr", [1, 256], F32)
    ag_in = nc.dram_tensor("ag_in", [128, 16], F32)
    ag_out = nc.dram_tensor("ag_out", [1024, 16], F32)

    PE, V, A, G, SY, SC = nc.tensor, nc.vector, nc.scalar, nc.gpsimd, nc.sync, nc.scalar

    with ExitStack() as es:
        e = es.enter_context

        ident = e(nc.sbuf_tensor("ident", [128, 128], BF16))
        e4t = e(nc.sbuf_tensor("e4t", [4, 128], BF16))
        eps_t = e(nc.sbuf_tensor("eps_t", [128, 1], F32))
        ones_b = e(nc.sbuf_tensor("ones_b", [128, 1], BF16))
        gw2_sb = e(nc.sbuf_tensor("gw2_sb", [128, 8], BF16))
        gb1B = e(nc.sbuf_tensor("gb1B_s", [128, 32], F32))
        gamB = e(nc.sbuf_tensor("gamB_s", [128, 32], F32))
        betB = e(nc.sbuf_tensor("betB_s", [128, 32], F32))
        gb2B = e(nc.sbuf_tensor("gb2B_s", [128, 8], F32))
        stats = [e(nc.sbuf_tensor(f"st{i}", [128, 96], F32)) for i in range(2)]
        mv = [e(nc.sbuf_tensor(f"mv{i}", [128, 2], F32)) for i in range(2)]
        nm = [e(nc.sbuf_tensor(f"nm{i}", [128, 1], F32)) for i in range(2)]
        nmr = [e(nc.sbuf_tensor(f"nmr{i}", [128, 1], F32)) for i in range(2)]
        rstd = [e(nc.sbuf_tensor(f"rs{i}", [128, 1], F32)) for i in range(2)]

        s_const = Sem(es, nc, "s_const")
        s_ld1 = Sem(es, nc, "s_ld1")
        s_msk = [Sem(es, nc, f"s_msk{i}") for i in range(2)]
        s_bin = Sem(es, nc, "s_bin")
        s_binp = Sem(es, nc, "s_binp")
        s_binpm = Sem(es, nc, "s_binpm")
        s_ldx = [Sem(es, nc, f"s_ldx{i}") for i in range(2)]
        s_abf = [Sem(es, nc, f"s_abf{i}") for i in range(2)]
        s_aused = Sem(es, nc, "s_aused")
        s_lnv = Sem(es, nc, "s_lnv")
        s_lna = Sem(es, nc, "s_lna")
        s_pt = Sem(es, nc, "s_pt")
        s_ev = Sem(es, nc, "s_ev")
        s_wm = [Sem(es, nc, f"s_wm{i}") for i in range(NWS)]
        s_wmf = [Sem(es, nc, f"s_wmf{i}") for i in range(NWS)]
        s_wa = [Sem(es, nc, f"s_wa{i}") for i in range(NWS)]
        s_mm = Sem(es, nc, "s_mm")
        s_aev = Sem(es, nc, "s_aev")
        s_stg = [Sem(es, nc, f"s_stg{i}") for i in range(8)]
        s_pb = [Sem(es, nc, f"s_pb{i}") for i in range(2)]
        s_px = [Sem(es, nc, f"s_px{i}") for i in range(2)]
        s_v4 = Sem(es, nc, "s_v4")
        s_a4 = Sem(es, nc, "s_a4")
        s_p4 = Sem(es, nc, "s_p4")
        s_pe4 = Sem(es, nc, "s_pe4")
        s_tp4 = Sem(es, nc, "s_tp4")
        s_l2u = Sem(es, nc, "s_l2u")
        s_x2s = [Sem(es, nc, f"s_x2s{i}") for i in range(2)]
        s_l2 = [Sem(es, nc, f"s_l2{i}") for i in range(2)]
        s_cci = Sem(es, nc, "s_cci")
        s_cco = Sem(es, nc, "s_cco")
        s_ccd = e(nc.semaphore("s_ccd"))
        s_bsi = Sem(es, nc, "s_bsi")
        s_bsd = Sem(es, nc, "s_bsd")
        s_ccd2 = e(nc.semaphore("s_ccd2"))
        s_q1 = Sem(es, nc, "s_q1")
        s_awx = Sem(es, nc, "s_awx")
        s_l2B = Sem(es, nc, "s_l2B")
        s_l2C = Sem(es, nc, "s_l2C")
        s_wfB = [Sem(es, nc, f"s_wfB{i}") for i in range(NWS)]
        s_wffB = [Sem(es, nc, f"s_wffB{i}") for i in range(NWS)]
        s_mmB = Sem(es, nc, "s_mmB")
        s_aevB = Sem(es, nc, "s_aevB")
        s_stgB = [Sem(es, nc, f"s_stgB{i}") for i in range(8)]
        s_peB = [Sem(es, nc, f"s_peB{i}") for i in range(2)]
        s_pxxB = [Sem(es, nc, f"s_pxxB{i}") for i in range(2)]
        s_v3B = Sem(es, nc, "s_v3B")
        s_a3B = Sem(es, nc, "s_a3B")
        s_p3B = Sem(es, nc, "s_p3B")
        s_outB = Sem(es, nc, "s_outB")

        blk = e(nc.Block())

        G.memset(ident[:, :], 0.0)
        G.affine_select(out=ident[:, :], in_=ident[:, :], compare_op=OP.not_equal,
                        fill=1.0, base=0, pattern=[[-1, 128]], channel_multiplier=1)
        G.memset(e4t[:, :], 1.0)
        G.affine_select(out=e4t[:, :], in_=e4t[:, :], compare_op=OP.is_ge,
                        fill=0.0, base=0, pattern=[[1, 128]], channel_multiplier=-32)
        G.affine_select(out=e4t[:, :], in_=e4t[:, :], compare_op=OP.is_ge,
                        fill=0.0, base=31, pattern=[[-1, 128]], channel_multiplier=32)
        s_const.inc(G.memset(ones_b[:, :], 1.0))
        s_const.inc(G.dma_start(out=gw2_sb[:, :], in_=gw2T_d), 16)
        s_ld1.inc(SY.dma_start(out=gb1B[:, :], in_=gb1B_d), 16)
        s_ld1.inc(SY.dma_start(out=gamB[:, :], in_=gamB_d), 16)
        s_ld1.inc(SY.dma_start(out=betB[:, :], in_=betB_d), 16)
        s_ld1.inc(SY.dma_start(out=gb2B[:, :], in_=gb2B_d), 16)
        V.memset(eps_t[:, :], EPS)
        V.drain()

        es_big = ExitStack()
        aT = es_big.enter_context(nc.sbuf_tensor("aT", [128, KC * TOK], BF16))
        binP = es_big.enter_context(nc.sbuf_tensor("binP", [128, 4096], BF16))
        bs_part = es_big.enter_context(nc.sbuf_tensor("bs_part", [128, 16], F32))
        bs_all = es_big.enter_context(nc.sbuf_tensor("bs_all", [128, 128], F32))
        binX = es_big.enter_context(nc.sbuf_tensor("binX", [128, 128], BF16))
        es_bst = ExitStack()

        es_msk = ExitStack()
        msk = [es_msk.enter_context(nc.sbuf_tensor(f"msk{i}", [128, 4096], BF16))
               for i in range(2)]

        with ExitStack() as es2:
            xbuf = [es2.enter_context(nc.sbuf_tensor(f"xb{i}", [128, D], F32))
                    for i in range(2)]
            abf = [es2.enter_context(nc.sbuf_tensor(f"ab{i}", [128, D], BF16))
                   for i in range(2)]
            ps_tp = [es2.enter_context(nc.psum_tensor(f"ps_tp{i}", [128, 1024], BF16))
                     for i in range(2)]

            aTv = aT[:, :].rearrange("p (kc t) -> p kc t", kc=KC)
            for tc in range(TC):
                sl = tc % 2
                XQ2 = SY if (tc < 8 or tc % 2 == 0) else G
                if tc >= 2:
                    XQ2.wait_ge(s_abf[sl].h, s_abf[sl].v)
                s_ldx[sl].inc(XQ2.dma_start(
                    out=xbuf[sl][:, :], in_=x_d[tc * 128:(tc + 1) * 128, :]), 16)
                V.wait_ge(s_ldx[sl].h, s_ldx[sl].v)
                if tc >= 2:
                    V.wait_ge(s_aused.h, tc - 1)
                ap = ln_chain(V, A, xbuf[sl][:, :], abf[sl][:, :], stats[sl], mv[sl],
                              nm[sl], nmr[sl], rstd[sl], eps_t, s_lnv, s_lna, 8)
                s_abf[sl].inc(ap)
                if tc == 0:
                    PE.wait_ge(s_const.h, s_const.v)
                PE.wait_ge(s_abf[sl].h, s_abf[sl].v)
                for kc in range(KC):
                    j = tc * KC + kc
                    bank = (j // 8) % 2
                    slot = j % 8
                    if j >= 16 and slot == 0:
                        PE.wait_ge(s_ev.h, j // 8 - 1)
                    s_pt.inc(PE.transpose(ps_tp[bank][:, slot * 128:(slot + 1) * 128],
                                          abf[sl][:, kc * 128:(kc + 1) * 128],
                                          ident[:, :]))
                    if kc == KC - 1:
                        s_aused.inc(PE.nop())
                    if slot == 7:
                        A.wait_ge(s_pt.h, s_pt.v)
                        s_ev.inc(A.copy(aTv[:, kc - 7:kc + 1, tc * 128:(tc + 1) * 128],
                                        ps_tp[bank][:, :].rearrange(
                                            "p (s t) -> p s t", s=8)))
                if tc < 4:
                    msl = tc % 2
                    if tc >= 2:
                        G.wait_ge(s_msk[msl].h, s_msk[msl].v)
                    s_msk[msl].inc(G.dma_start(
                        out=msk[msl][:, :].rearrange("p (a j) -> p a j", a=4),
                        in_=maskR_d[:, 4 * tc:4 * (tc + 1), :]), 16)
                    V.wait_ge(s_msk[msl].h, s_msk[msl].v)
                    s_msk[msl].inc(V.tensor_reduce(
                        bs_part[:, 4 * tc:4 * (tc + 1)],
                        msk[msl][:, :].rearrange("p (a j) -> p a j", a=4),
                        axis=AX, op=OP.add))
                    V.drain()
                if tc == 3:
                    SC.wait_ge(s_msk[0].h, s_msk[0].v)
                    SC.wait_ge(s_msk[1].h, s_msk[1].v)
                    s_bsi.inc(SC.dma_start(out=ag_in.ap(), in_=bs_part[:, :]), 16)
                if tc == 4:
                    G.wait_ge(s_bsi.h, s_bsi.v)
                    G.collective_compute(
                        "AllGather", OP.bypass,
                        replica_groups=[list(range(NCORES))],
                        ins=[ag_in.ap().opt()], outs=[ag_out.ap().opt()],
                    ).then_inc(s_ccd2)
                if tc == 6:
                    SC.wait_ge(s_ccd2, 1)
                    s_bsd.inc(SC.dma_start(
                        out=bs_all[:, :].rearrange("p (r j) -> p r j", r=8),
                        in_=ag_out.ap().rearrange("(r p) j -> p r j", r=8)), 16)

        V.wait_ge(s_bsd.h, s_bsd.v)
        s_bin.inc(V.tensor_scalar(binX[:, :], bs_all[:, :], 0.0, None, OP.is_gt))
        SC.wait_ge(s_bin.h, s_bin.v)
        s_bin.inc(SC.dma_start(out=bin_dram.ap(), in_=binX[:, :]), 16)
        with ExitStack() as esb:
            bin_g = esb.enter_context(nc.sbuf_tensor("bin_g", [4, 4096], BF16))
            ps_bin = [esb.enter_context(nc.psum_tensor(f"ps_bin{i}", [128, 512], F32))
                      for i in range(2)]
            SY.wait_ge(s_bin.h, s_bin.v)
            s_bin.inc(SY.dma_start(
                out=bin_g[:, :].rearrange("g (kc nb) -> g kc nb", kc=KC),
                in_=bin_dram.ap().rearrange("(g kc) nb -> g kc nb", g=4)), 16)
            PE.wait_ge(s_bin.h, s_bin.v)
            bgv = bin_g[:, :].rearrange("g (kc nb) -> g kc nb", kc=KC)
            for nt in range(NTN):
                if nt >= 2:
                    PE.wait_ge(s_binp.h, nt - 1)
                s_binpm.inc(PE.matmul(ps_bin[nt % 2][:, 0:512], e4t[:, :],
                                      bgv[:, :, nt * 16:(nt + 1) * 16],
                                      start=True, stop=True))
                A.wait_ge(s_binpm.h, s_binpm.v)
                s_binp.inc(A.copy(binP[:, nt * 512:(nt + 1) * 512],
                                  ps_bin[nt % 2][:, 0:512]))
        es_msk.close()

        es_t14 = ExitStack()
        t1_sb = es_t14.enter_context(nc.sbuf_tensor("t1_sb", [128, TOK], BF16,
                                                    side="right"))
        t2_sb = es_t14.enter_context(nc.sbuf_tensor("t2_sb", [128, TOK], BF16,
                                                    side="right"))
        bstats = es_bst.enter_context(
            nc.sbuf_tensor("bstats", [128, TC * NTN * 6], F32, side="right"))
        with ExitStack() as es3:
            wm = [es3.enter_context(nc.sbuf_tensor(f"wm{i}", [128, KH * NTW], BF16))
                  for i in range(NWS)]
            stage = [es3.enter_context(nc.sbuf_tensor(f"stg{i}", [128, NTW], BF16))
                     for i in range(8)]
            ps_acc = [es3.enter_context(nc.psum_tensor(f"ps_acc{i}", [128, 512], F32))
                      for i in range(8)]

            grp = 0
            for nt in range(NTN):
                for kh in range(2):
                    sidx = nt * 2 + kh
                    wsl = sidx % NWS
                    if sidx == 0:
                        G.wait_ge(s_binp.h, s_binp.v)
                        G.wait_ge(s_msk[0].h, s_msk[0].v)
                        G.wait_ge(s_msk[1].h, s_msk[1].v)
                    if sidx >= NWS:
                        G.wait_ge(s_wmf[wsl].h, s_wmf[wsl].v)
                    s_wa[wsl].inc(G.dma_start(
                        out=wm[wsl][:, :].rearrange("p (kc n) -> p kc n", kc=KH),
                        in_=waT_d[kh * KH * 128:(kh + 1) * KH * 128,
                                  nt * NTW:(nt + 1) * NTW]
                            .rearrange("(kc kp) n -> kp kc n", kp=128)), 16)
                    APL = G
                    APL.wait_ge(s_wa[wsl].h, s_wa[wsl].v)
                    APL.wait_ge(s_binp.h, s_binp.v)
                    bview = (binP[:, nt * 512 + kh * 256:nt * 512 + (kh + 1) * 256]
                             .rearrange("p (kc nb) -> p kc nb", kc=KH)
                             .unsqueeze(3).broadcast_to([128, KH, 16, 32]))
                    wview = wm[wsl][:, :].rearrange("p (kc nb n) -> p kc nb n",
                                                    kc=KH, nb=16)
                    s_wm[wsl].inc(APL.tensor_tensor(wview, wview, bview, OP.mult))

                for tg in range(TG):
                    bank = (grp % 2) * 4
                    if grp == 0:
                        PE.wait_ge(s_ev.h, s_ev.v)
                        PE.wait_ge(s_binp.h, s_binp.v)
                    if grp >= 2:
                        PE.wait_ge(s_aev.h, 4 * (grp - 1))
                    mmi = None
                    for kc in range(KC):
                        wsl = (nt * 2 + kc // KH) % NWS
                        if kc % KH == 0:
                            PE.wait_ge(s_wm[wsl].h, s_wm[wsl].v)
                        for tci in range(4):
                            tc = tg * 4 + tci
                            mmi = PE.matmul(
                                ps_acc[bank + tci][:, 0:NTW],
                                aT[:, kc * TOK + tc * 128:kc * TOK + (tc + 1) * 128],
                                wm[wsl][:, (kc % KH) * NTW:(kc % KH + 1) * NTW],
                                start=(kc == 0), stop=(kc == KC - 1))
                    s_mm.inc(mmi)
                    if tg == TG - 1:
                        s_wmf[(nt * 2) % NWS].inc(PE.nop())
                        s_wmf[(nt * 2 + 1) % NWS].inc(PE.nop())
                    A.wait_ge(s_mm.h, s_mm.v)
                    for tci in range(4):
                        tc = tg * 4 + tci
                        ssl = (grp % 2) * 4 + tci
                        uses = grp // 2
                        if uses >= 1:
                            A.wait_ge(s_stg[ssl].h, 17 * uses)
                        s_aev.inc(A.copy(stage[ssl][:, :], ps_acc[bank + tci][:, 0:NTW]))
                        SC.wait_ge(s_aev.h, s_aev.v)
                        s_stg[ssl].inc(SC.dma_start(
                            out=b_dram.ap()[tc * 128:(tc + 1) * 128,
                                            nt * NTW:(nt + 1) * NTW],
                            in_=stage[ssl][:, :]), 16)
                        V.wait_ge(s_aev.h, s_aev.v)
                        s_stg[ssl].inc(V.bn_stats(
                            bstats[:, (tc * NTN + nt) * 6:(tc * NTN + nt) * 6 + 6],
                            stage[ssl][:, :]))
                    grp += 1
        es_big.close()

        with ExitStack() as es4:
            gw1_sb = es4.enter_context(nc.sbuf_tensor("gw1_sb", [128, KC * 32], BF16))
            A_sb = es4.enter_context(nc.sbuf_tensor("A_sb", [128, KC * 128], BF16))
            mvb = es4.enter_context(nc.sbuf_tensor("mvb", [128, 2 * TC], F32))
            rsb = es4.enter_context(nc.sbuf_tensor("rsb", [128, TC], F32))
            nmb = es4.enter_context(nc.sbuf_tensor("nmb", [128, TC], F32))
            h_all = es4.enter_context(nc.sbuf_tensor("h_all", [128, TC * 32], F32))
            hsq = es4.enter_context(nc.sbuf_tensor("hsq", [128, TC * 32], F32))
            hmu = es4.enter_context(nc.sbuf_tensor("hmu", [128, TC], F32))
            hvar = es4.enter_context(nc.sbuf_tensor("hvar", [128, TC], F32))
            hrs = es4.enter_context(nc.sbuf_tensor("hrs", [128, TC], F32))
            hb_all = es4.enter_context(nc.sbuf_tensor("hb_all", [128, TC * 32], BF16))
            hT_all = es4.enter_context(nc.sbuf_tensor("hT_all", [32, TC * 128], BF16))
            gat_all = es4.enter_context(nc.sbuf_tensor("gat_all", [128, TC * 8], F32))
            gat2_all = es4.enter_context(nc.sbuf_tensor("gat2_all", [128, TC * 8], F32))
            m1a = es4.enter_context(nc.sbuf_tensor("m1a", [128, TC], F32))
            m2a = es4.enter_context(nc.sbuf_tensor("m2a", [128, TC], F32))
            dga = es4.enter_context(nc.sbuf_tensor("dga", [128, TC], F32))
            sxa = es4.enter_context(nc.sbuf_tensor("sxa", [128, TC], F32))
            spa = es4.enter_context(nc.sbuf_tensor("spa", [128, TC], F32))
            rsa = es4.enter_context(nc.sbuf_tensor("rsa", [128, TC], F32))
            p1a = es4.enter_context(nc.sbuf_tensor("p1a", [128, TC], F32))
            p2a = es4.enter_context(nc.sbuf_tensor("p2a", [128, TC], F32))
            eq1a = es4.enter_context(nc.sbuf_tensor("eq1a", [128, TC * 8], F32))
            eq2a = es4.enter_context(nc.sbuf_tensor("eq2a", [128, TC * 8], F32))
            g1a = es4.enter_context(nc.sbuf_tensor("g1a", [128, TC * 8], F32))
            g2a = es4.enter_context(nc.sbuf_tensor("g2a", [128, TC * 8], F32))
            eqb_all = es4.enter_context(nc.sbuf_tensor("eqb_all", [128, TC * 16], BF16))
            tmpf_all = es4.enter_context(nc.sbuf_tensor("tmpf_all", [128, TC * 128], F32))
            t1b_all = es4.enter_context(nc.sbuf_tensor("t1b_all", [128, TC * 128], BF16))
            t2b_all = es4.enter_context(nc.sbuf_tensor("t2b_all", [128, TC * 128], BF16))
            cntf = es4.enter_context(nc.sbuf_tensor("cntf", [1, 256], F32))
            cnt_row = es4.enter_context(nc.sbuf_tensor("cnt_row", [1, 16], F32))
            ps_tp4 = [es4.enter_context(nc.psum_tensor(f"ps_tp4{i}", [128, 1024], BF16))
                      for i in range(2)]
            ps_h = es4.enter_context(nc.psum_tensor("ps_h", [128, 512], F32))
            ps_tmp = es4.enter_context(nc.psum_tensor("ps_tmp", [128, 512], F32))
            ps_hT4 = es4.enter_context(nc.psum_tensor("ps_hT4", [32, TC * 128], BF16))
            ps_g = es4.enter_context(nc.psum_tensor("ps_g", [128, 128], F32))
            ps_cnt2 = es4.enter_context(nc.psum_tensor("ps_cnt2", [1, 256], F32))
            bbuf = [es4.enter_context(nc.sbuf_tensor(f"bb{i}", [128, D], BF16))
                    for i in range(2)]
            xbuf4 = [es4.enter_context(nc.sbuf_tensor(f"xb4{i}", [128, D], F32))
                     for i in range(2)]
            x2_sb = [es4.enter_context(nc.sbuf_tensor(f"x2_{i}", [128, D], F32))
                     for i in range(2)]
            l2bf = [es4.enter_context(nc.sbuf_tensor(f"l2b{i}", [128, D], BF16))
                    for i in range(2)]
            l2t = [es4.enter_context(nc.sbuf_tensor(f"l2t{i}", [128, D], BF16))
                   for i in range(2)]

            G.wait_ge(s_aev.h, s_aev.v)
            s_const.inc(G.dma_start(
                out=gw1_sb[:, :].rearrange("p (kc g) -> p kc g", kc=KC),
                in_=gw1T_d.rearrange("(kc kp) g -> kp kc g", kp=128)), 16)
            s_const.inc(G.dma_start(
                out=A_sb[:, :].rearrange("p (kc er) -> p kc er", kc=KC),
                in_=aRe_d.rearrange("(kc kp) er -> kp kc er", kp=128)), 16)

            SY.wait_ge(s_aev.h, s_aev.v)
            for i in range(8):
                SY.wait_ge(s_stg[i].h, s_stg[i].v)

            for tc in range(TC):
                V.bn_aggr(mvb[:, 2 * tc:2 * tc + 2], bstats[:, tc * 48:(tc + 1) * 48])
                V.drain()
            s_v4.inc(V.tensor_scalar(nmb[:, :],
                                     mvb[:, :].rearrange("p (t two) -> p t two", two=2)[:, :, 0:1].squeeze(2),
                                     -1.0, None, OP.mult))
            A.wait_ge(s_v4.h, s_v4.v)
            A.activation(rsb[:, :],
                         mvb[:, :].rearrange("p (t two) -> p t two", two=2)[:, :, 1:2].squeeze(2),
                         AF.Ln, bias=eps_t[:, 0:1], scale=1.0)
            A.drain()
            s_a4.inc(A.activation(rsb[:, :], rsb[:, :], AF.Exp, scale=-0.5))
            V.wait_ge(s_a4.h, s_a4.v)
            s_v4.inc(V.tensor_tensor(nmb[:, :], nmb[:, :], rsb[:, :], OP.mult))
            V.drain()
            es_bst.close()

            a4_gate_prev = 0
            ptc = 0
            stt_v4 = [0, 0]
            ln_a4 = [0, 0]
            for step in range(TC + 1):
                if step < TC:
                    tc = step
                    sl = tc % 2
                    if tc >= 1:
                        psl = (tc - 1) % 2
                        SQ4 = SY if psl == 0 else SC
                        SQ4.wait_ge(s_v4.h, stt_v4[psl])
                        s_x2s[psl].inc(SQ4.dma_start(
                            out=x2_d[(tc - 1) * 128:tc * 128, :],
                            in_=x2_sb[psl][:, :]), 16)
                    if tc >= 2:
                        SY.wait_ge(s_v4.h, stt_v4[sl])
                        G.wait_ge(s_v4.h, stt_v4[sl])
                    s_pb[sl].inc(SY.dma_start(
                        out=bbuf[sl][:, :],
                        in_=b_dram.ap()[tc * 128:(tc + 1) * 128, :]), 16)
                    s_px[sl].inc(G.dma_start(
                        out=xbuf4[sl][:, :], in_=x_d[tc * 128:(tc + 1) * 128, :]), 16)
                    V.wait_ge(s_pb[sl].h, s_pb[sl].v)
                    V.wait_ge(s_px[sl].h, s_px[sl].v)
                    if tc >= 2:
                        V.wait_ge(s_x2s[sl].h, s_x2s[sl].v)
                    V.tensor_scalar(xbuf4[sl][:, :], xbuf4[sl][:, :],
                                    nmb[:, tc:tc + 1], None, OP.add)
                    V.drain()
                    s_v4.inc(V.scalar_tensor_tensor(
                        x2_sb[sl][:, :], bbuf[sl][:, :], rsb[:, tc:tc + 1],
                        xbuf4[sl][:, :], OP.mult, OP.add))
                    stt_v4[sl] = s_v4.v
                    for i in range(8):
                        V.bn_stats(stats[sl][:, 6 * i:6 * i + 6],
                                   x2_sb[sl][:, 512 * i:512 * (i + 1)])
                    V.drain()
                    V.bn_aggr(mv[sl][:, :], stats[sl][:, 0:48])
                    V.drain()
                    s_v4.inc(V.tensor_scalar(nm[sl][:, :], mv[sl][:, 0:1], -1.0,
                                             None, OP.mult))
                    A.wait_ge(s_v4.h, s_v4.v)
                    A.activation(rstd[sl][:, :], mv[sl][:, 1:2], AF.Ln,
                                 bias=eps_t[:, 0:1], scale=1.0)
                    A.drain()
                    s_a4.inc(A.activation(rstd[sl][:, :], rstd[sl][:, :], AF.Exp,
                                          scale=-0.5))
                    ln_a4[sl] = s_a4.v
                if step < 1:
                    continue
                tc = step - 1
                sl = tc % 2
                V.wait_ge(s_a4.h, ln_a4[sl])
                if tc >= 2:
                    V.wait_ge(s_l2u.h, tc - 1)
                V.tensor_tensor(nmr[sl][:, :], nm[sl][:, :], rstd[sl][:, :], OP.mult)
                V.drain()
                s_v4.inc(V.tensor_scalar(l2bf[sl][:, :], x2_sb[sl][:, :],
                                         rstd[sl][:, 0:1], nmr[sl][:, 0:1],
                                         OP.mult, OP.add))

                PE.wait_ge(s_v4.h, s_v4.v)
                if tc >= 2:
                    PE.wait_ge(s_l2[sl].h, s_l2[sl].v)
                for kc in range(KC):
                    bank = (ptc // 8) % 2
                    slot = ptc % 8
                    if slot == 0 and ptc >= 16:
                        PE.wait_ge(s_tp4.h, ptc // 8 - 1)
                    s_pe4.inc(PE.transpose(
                        ps_tp4[bank][:, slot * 128:(slot + 1) * 128],
                        l2bf[sl][:, kc * 128:(kc + 1) * 128], ident[:, :]))
                    if kc == KC - 1:
                        s_l2u.inc(PE.nop())
                    if slot == 7:
                        A.wait_ge(s_pe4.h, s_pe4.v)
                        s_tp4.inc(A.copy(
                            l2t[sl][:, (kc - 7) * 128:(kc + 1) * 128],
                            ps_tp4[bank][:, 0:1024]))
                    ptc += 1
                SC.wait_ge(s_tp4.h, s_tp4.v)
                s_l2[sl].inc(SC.dma_start(
                    out=l2t_d[tc * 128:(tc + 1) * 128, :], in_=l2t[sl][:, :]), 16)

                PE.wait_ge(s_tp4.h, s_tp4.v)
                PE.wait_ge(s_a4.h, a4_gate_prev)
                PE.wait_ge(s_const.h, s_const.v)
                mmi = None
                for kc in range(KC):
                    blkap = l2t[sl][:, kc * 128:(kc + 1) * 128]
                    PE.matmul(ps_h[:, 0:32], blkap, gw1_sb[:, kc * 32:(kc + 1) * 32],
                              start=(kc == 0), stop=(kc == KC - 1))
                    mmi = PE.matmul(ps_tmp[:, 0:128], blkap,
                                    A_sb[:, kc * 128:(kc + 1) * 128],
                                    start=(kc == 0), stop=(kc == KC - 1))
                s_pe4.inc(mmi)
                s_l2[sl].inc(PE.nop())
                A.wait_ge(s_pe4.h, s_pe4.v)
                A.copy(h_all[:, tc * 32:(tc + 1) * 32], ps_h[:, 0:32])
                A.drain()
                s_a4.inc(A.copy(tmpf_all[:, tc * 128:(tc + 1) * 128],
                                ps_tmp[:, 0:128]))
                a4_gate_prev = s_a4.v

            SC.wait_ge(s_v4.h, stt_v4[1])
            s_x2s[1].inc(SC.dma_start(
                out=x2_d[(TC - 1) * 128:TC * 128, :], in_=x2_sb[1][:, :]), 16)
            loopend = {s: s.v for s in (s_v4, s_a4, s_pe4,
                                        s_l2[0], s_l2[1], s_x2s[0], s_x2s[1])}

            hv = h_all[:, :].rearrange("p (c g) -> p c g", c=TC)
            hqv = hsq[:, :].rearrange("p (c g) -> p c g", c=TC)
            gb1v = gb1B[:, :].unsqueeze(1).broadcast_to([128, TC, 32])
            gamv = gamB[:, :].unsqueeze(1).broadcast_to([128, TC, 32])
            betv = betB[:, :].unsqueeze(1).broadcast_to([128, TC, 32])
            V.wait_ge(s_a4.h, s_a4.v)
            V.tensor_tensor(hv, hv, gb1v, OP.add)
            V.drain()
            V.tensor_reduce(hmu[:, :], hv, axis=AX, op=OP.add)
            V.drain()
            V.tensor_scalar(hmu[:, :], hmu[:, :], -1.0 / 32.0, None, OP.mult)
            V.drain()
            V.tensor_tensor(hv, hv,
                            hmu[:, :].unsqueeze(2).broadcast_to([128, TC, 32]),
                            OP.add)
            V.drain()
            V.tensor_tensor(hqv, hv, hv, OP.mult)
            V.drain()
            V.tensor_reduce(hvar[:, :], hqv, axis=AX, op=OP.add)
            V.drain()
            s_v4.inc(V.tensor_scalar(hvar[:, :], hvar[:, :], 1.0 / 32.0,
                                     None, OP.mult))
            A.wait_ge(s_v4.h, s_v4.v)
            A.activation(hrs[:, :], hvar[:, :], AF.Ln, bias=eps_t[:, 0:1], scale=1.0)
            A.drain()
            s_a4.inc(A.activation(hrs[:, :], hrs[:, :], AF.Exp, scale=-0.5))
            V.wait_ge(s_a4.h, s_a4.v)
            V.tensor_tensor(hv, hv,
                            hrs[:, :].unsqueeze(2).broadcast_to([128, TC, 32]),
                            OP.mult)
            V.drain()
            V.tensor_tensor(hv, hv, gamv, OP.mult)
            V.drain()
            V.tensor_tensor(hv, hv, betv, OP.add)
            V.drain()
            V.tensor_scalar(h_all[:, :], h_all[:, :], 0.0, None, OP.max)
            V.drain()
            s_v4.inc(V.tensor_copy(hb_all[:, :], h_all[:, :]))
            PE.wait_ge(s_v4.h, s_v4.v)
            for tc in range(TC):
                s_pe4.inc(PE.transpose(ps_hT4[:, tc * 128:(tc + 1) * 128],
                                       hb_all[:, tc * 32:(tc + 1) * 32],
                                       ident[:, :]))
            A.wait_ge(s_pe4.h, s_pe4.v)
            s_a4.inc(A.copy(hT_all[:, :], ps_hT4[:, :]))
            PE.wait_ge(s_a4.h, s_a4.v)
            mmi = None
            for tc in range(TC):
                mmi = PE.matmul(ps_g[:, tc * 8:(tc + 1) * 8],
                                hT_all[0:32, tc * 128:(tc + 1) * 128],
                                gw2_sb[0:32, :],
                                start=True, stop=True)
            s_pe4.inc(mmi)
            A.wait_ge(s_pe4.h, s_pe4.v)
            s_a4.inc(A.copy(gat_all[:, :], ps_g[:, 0:128]))
            gv = gat_all[:, :].rearrange("p (c e) -> p c e", c=TC)
            gv2 = gat2_all[:, :].rearrange("p (c e) -> p c e", c=TC)
            e1v = eq1a[:, :].rearrange("p (c e) -> p c e", c=TC)
            e2v = eq2a[:, :].rearrange("p (c e) -> p c e", c=TC)
            V.wait_ge(s_a4.h, s_a4.v)
            V.tensor_tensor(gv, gv,
                            gb2B[:, :].unsqueeze(1).broadcast_to([128, TC, 8]),
                            OP.add)
            V.drain()
            V.reduce_max(m1a[:, :], gv, axis=AX)
            V.drain()
            V.tensor_tensor(e1v, gv,
                            m1a[:, :].unsqueeze(2).broadcast_to([128, TC, 8]),
                            OP.is_equal)
            V.drain()
            V.scalar_tensor_tensor(gv2, e1v, BIG_NEG, gv, OP.mult, OP.add)
            V.drain()
            V.reduce_max(m2a[:, :], gv2, axis=AX)
            V.drain()
            V.tensor_tensor(e2v, gv2,
                            m2a[:, :].unsqueeze(2).broadcast_to([128, TC, 8]),
                            OP.is_equal)
            V.drain()
            ebv = eqb_all[:, :].rearrange("p (c k) -> p c k", c=TC)
            V.tensor_copy(ebv[:, :, 0:8], e1v)
            V.drain()
            s_v4.inc(V.tensor_copy(ebv[:, :, 8:16], e2v))
            PE.wait_ge(s_v4.h, s_v4.v)
            s_pe4.inc(PE.matmul(ps_cnt2[0:1, 0:256], ones_b[:, :], eqb_all[:, :],
                                start=True, stop=True))
            A.wait_ge(s_pe4.h, s_pe4.v)
            s_a4.inc(A.copy(cntf[:, :], ps_cnt2[0:1, 0:256]))
            V.wait_ge(s_a4.h, s_a4.v)
            s_v4.inc(V.tensor_reduce(
                cnt_row[:, :], cntf[:, :].rearrange("p (c k) -> p k c", c=TC),
                axis=AX, op=OP.add))
            cnt_v4 = s_v4.v
            G.wait_ge(s_v4.h, cnt_v4)
            s_cci.inc(G.dma_start(out=cc_in.ap(), in_=cnt_row[:, :]), 16)
            G.wait_ge(s_cci.h, s_cci.v)
            G.collective_compute(
                "AllReduce", OP.add, replica_groups=[list(range(NCORES))],
                ins=[cc_in.ap().opt()], outs=[cc_out.ap().opt()],
            ).then_inc(s_ccd)
            s_v4.inc(V.scalar_tensor_tensor(dga[:, :], m1a[:, :], -1.0, m2a[:, :],
                                            OP.mult, OP.add))
            A.wait_ge(s_v4.h, s_v4.v)
            s_a4.inc(A.activation(sxa[:, :], dga[:, :], AF.Exp))
            V.wait_ge(s_a4.h, s_a4.v)
            V.tensor_scalar(spa[:, :], sxa[:, :], 1.0, None, OP.add)
            V.drain()
            V.reciprocal(rsa[:, :], spa[:, :])
            V.drain()
            V.tensor_tensor(p2a[:, :], sxa[:, :], rsa[:, :], OP.mult)
            V.drain()
            V.tensor_scalar(p1a[:, :], p2a[:, :], -ALPHA, ALPHA, OP.mult, OP.add)
            V.drain()
            V.tensor_scalar(p2a[:, :], p2a[:, :], ALPHA, None, OP.mult)
            V.drain()
            V.tensor_tensor(g1a[:, :].rearrange("p (c e) -> p c e", c=TC), e1v,
                            p1a[:, :].unsqueeze(2).broadcast_to([128, TC, 8]),
                            OP.mult)
            V.drain()
            V.tensor_tensor(g2a[:, :].rearrange("p (c e) -> p c e", c=TC), e2v,
                            p2a[:, :].unsqueeze(2).broadcast_to([128, TC, 8]),
                            OP.mult)
            V.drain()
            tv = tmpf_all[:, :].rearrange("p (c ee r) -> p c ee r", c=TC, ee=8)
            V.tensor_tensor(
                t1b_all[:, :].rearrange("p (c ee r) -> p c ee r", c=TC, ee=8), tv,
                g1a[:, :].rearrange("p (c e) -> p c e", c=TC)
                .unsqueeze(3).broadcast_to([128, TC, 8, 16]), OP.mult)
            V.drain()
            s_v4.inc(V.tensor_tensor(
                t2b_all[:, :].rearrange("p (c ee r) -> p c ee r", c=TC, ee=8), tv,
                g2a[:, :].rearrange("p (c e) -> p c e", c=TC)
                .unsqueeze(3).broadcast_to([128, TC, 8, 16]), OP.mult))
            tp_base = s_tp4.v
            PE.wait_ge(s_v4.h, s_v4.v)
            PE.wait_ge(s_tp4.h, tp_base)
            tpi = 0
            for tsrc, tdst in ((t1b_all, t1_sb), (t2b_all, t2_sb)):
                for tc in range(TC):
                    bank = (tpi // 8) % 2
                    slot = tpi % 8
                    if slot == 0 and tpi >= 16:
                        PE.wait_ge(s_tp4.h, tp_base + tpi // 8 - 1)
                    s_pe4.inc(PE.transpose(
                        ps_tp4[bank][:, slot * 128:(slot + 1) * 128],
                        tsrc[:, tc * 128:(tc + 1) * 128], ident[:, :]))
                    if slot == 7:
                        A.wait_ge(s_pe4.h, s_pe4.v)
                        s_tp4.inc(A.copy(tdst[:, (tc - 7) * 128:(tc + 1) * 128],
                                         ps_tp4[bank][:, 0:1024]))
                    tpi += 1

        es_bflat = ExitStack()
        ebf = es_bflat.enter_context
        tot = ebf(nc.sbuf_tensor("tot", [1, 16], F32))
        alw = ebf(nc.sbuf_tensor("alw", [1, 16], F32))
        alw_rep = ebf(nc.sbuf_tensor("alw_rep", [1, 256], F32))
        a0x = ebf(nc.sbuf_tensor("a0x", [128, 1], F32))
        a1x = ebf(nc.sbuf_tensor("a1x", [128, 1], F32))
        estats = ebf(nc.sbuf_tensor("estats", [128, TC * NTN * 6], F32))
        tall = ebf(nc.sbuf_tensor("tall", [128, TOK], BF16))

        es_q = ExitStack()
        b_sb = es_q.enter_context(nc.sbuf_tensor("b_sb", [128, D], BF16))
        l2T = es_q.enter_context(nc.sbuf_tensor("l2T", [128, KC * TOK], BF16))

        l2Tv2 = l2T[:, :].rearrange("p (kc t) -> p kc t", kc=KC)
        for Q in (SY, SC):
            Q.wait_ge(s_v4.h, s_v4.v)
            Q.wait_ge(s_a4.h, s_a4.v)
            Q.wait_ge(s_pe4.h, s_pe4.v)
            Q.wait_ge(s_l2[0].h, s_l2[0].v)
            Q.wait_ge(s_l2[1].h, s_l2[1].v)
            Q.wait_ge(s_x2s[0].h, s_x2s[0].v)
            Q.wait_ge(s_x2s[1].h, s_x2s[1].v)

        with ExitStack() as es3:
            wf = [es3.enter_context(nc.sbuf_tensor(f"wf{i}", [128, KH * NTW], BF16))
                  for i in range(NWS)]
            ps_acc = [es3.enter_context(nc.psum_tensor(f"ps_accB{i}", [128, 512], F32))
                      for i in range(8)]

            def wf_load(sidx):
                kh = sidx % 2
                nt = sidx // 2
                wsl = sidx % NWS
                if sidx >= NWS:
                    G.wait_ge(s_wffB[wsl].h, s_wffB[wsl].v)
                s_wfB[wsl].inc(G.dma_start(
                    out=wf[wsl][:, :].rearrange("p (kc n) -> p kc n", kc=KH),
                    in_=wfT_d[kh * KH * 128:(kh + 1) * KH * 128,
                              nt * NTW:(nt + 1) * NTW]
                        .rearrange("(kc kp) n -> kp kc n", kp=128)), 16)

            G.wait_ge(s_mm.h, s_mm.v)
            G.wait_ge(s_aev.h, s_aev.v)
            for i in range(8):
                G.wait_ge(s_stg[i].h, s_stg[i].v)
            for s, v in loopend.items():
                G.wait_ge(s.h, v)
            wf_load(0)
            wf_load(1)
            s_q1.inc(G.dma_start(out=b_sb[:, :], in_=bRe_d), 16)
            wf_load(2)
            G.wait_ge(s_ccd, 1)
            s_cco.inc(G.dma_start(out=tot[:, :], in_=cc_out.ap()), 16)

            l2mark = [None] * TC
            for tc in range(TC):
                Q, sem = (SY, s_l2B) if tc % 2 == 0 else (SC, s_l2C)
                sem.inc(Q.dma_start(
                    out=l2Tv2[:, :, tc * 128:(tc + 1) * 128],
                    in_=l2t_d[tc * 128:(tc + 1) * 128, :]
                        .rearrange("p (kc t) -> p kc t", kc=KC)), 16)
                l2mark[tc] = (sem, sem.v)

            V.wait_ge(s_cco.h, s_cco.v)
            V.tensor_scalar(alw[:, :], tot[:, :], CAPACITY + 0.5, None, OP.is_lt)
            V.drain()
            s_v3B.inc(V.tensor_copy(
                alw_rep[:, :].rearrange("p (k e r) -> p k e r", k=2, e=8),
                alw[:, :].rearrange("p (k e) -> p k e", k=2)
                .unsqueeze(3).broadcast_to([1, 2, 8, 16])))
            G.wait_ge(s_v3B.h, s_v3B.v)
            s_awx.inc(G.dma_start(out=awd_dram.ap(), in_=alw_rep[:, :]), 16)
            G.wait_ge(s_awx.h, 16)
            awd_col = awd_dram.ap().rearrange("o (a b) -> (o a) b", b=1)
            s_awx.inc(G.dma_start(out=a0x[:, :], in_=awd_col[0:128, :]), 16)
            s_awx.inc(G.dma_start(out=a1x[:, :], in_=awd_col[128:256, :]), 16)
            V.wait_ge(s_awx.h, s_awx.v)
            V.wait_ge(s_tp4.h, s_tp4.v)
            V.tensor_scalar(tall[:, :], t2_sb[:, :], a1x[:, 0:1], None, OP.mult)
            V.drain()
            s_v3B.inc(V.scalar_tensor_tensor(tall[:, :], t1_sb[:, :], a0x[:, 0:1],
                                             tall[:, :], OP.mult, OP.add))
            es_t14.close()
            stage = [es3.enter_context(nc.sbuf_tensor(f"stgB{i}", [128, NTW], BF16))
                     for i in range(8)]

            grp = 0
            for nt in range(NTN):
                for kh in range(2):
                    sidx = nt * 2 + kh
                    if sidx >= 3:
                        wf_load(sidx)
                for tg in range(TG):
                    bank = (grp % 2) * 4
                    if nt == 0:
                        for tci in range(4):
                            sem, val = l2mark[tg * 4 + tci]
                            PE.wait_ge(sem.h, val)
                    if grp == 0:
                        PE.wait_ge(s_q1.h, s_q1.v)
                    if grp >= 2:
                        PE.wait_ge(s_aevB.h, 4 * (grp - 1))
                    for kc in range(KC):
                        wsl = (nt * 2 + kc // KH) % NWS
                        if kc % KH == 0:
                            PE.wait_ge(s_wfB[wsl].h, s_wfB[wsl].v)
                        for tci in range(4):
                            tc = tg * 4 + tci
                            PE.matmul(
                                ps_acc[bank + tci][:, 0:NTW],
                                l2T[:, kc * TOK + tc * 128:kc * TOK + (tc + 1) * 128],
                                wf[wsl][:, (kc % KH) * NTW:(kc % KH + 1) * NTW],
                                start=(kc == 0), stop=False)
                    if grp == 0:
                        PE.wait_ge(s_v3B.h, s_v3B.v)
                    mmi = None
                    for tci in range(4):
                        tc = tg * 4 + tci
                        mmi = PE.matmul(ps_acc[bank + tci][:, 0:NTW],
                                        tall[:, tc * 128:(tc + 1) * 128],
                                        b_sb[:, nt * NTW:(nt + 1) * NTW],
                                        start=False, stop=True)
                    s_mmB.inc(mmi)
                    if tg == TG - 1:
                        s_wffB[(nt * 2) % NWS].inc(PE.nop())
                        s_wffB[(nt * 2 + 1) % NWS].inc(PE.nop())
                    A.wait_ge(s_mmB.h, s_mmB.v)
                    for tci in range(4):
                        tc = tg * 4 + tci
                        ssl = (grp % 2) * 4 + tci
                        uses = grp // 2
                        if uses >= 1:
                            A.wait_ge(s_stgB[ssl].h, 17 * uses)
                        s_aevB.inc(A.copy(stage[ssl][:, :], ps_acc[bank + tci][:, 0:NTW]))
                        SC.wait_ge(s_aevB.h, s_aevB.v)
                        s_stgB[ssl].inc(SC.dma_start(
                            out=e_dram.ap()[tc * 128:(tc + 1) * 128,
                                            nt * NTW:(nt + 1) * NTW],
                            in_=stage[ssl][:, :]), 16)
                        V.wait_ge(s_aevB.h, s_aevB.v)
                        s_stgB[ssl].inc(V.bn_stats(
                            estats[:, (tc * NTN + nt) * 6:(tc * NTN + nt) * 6 + 6],
                            stage[ssl][:, :]))
                    grp += 1
        es_q.close()

        with ExitStack() as es4B:
            ebuf = [es4B.enter_context(nc.sbuf_tensor(f"eb{i}", [128, D], BF16))
                    for i in range(4)]
            x2b = [es4B.enter_context(nc.sbuf_tensor(f"x2b{i}", [128, D], F32))
                   for i in range(4)]
            o_sb = [es4B.enter_context(nc.sbuf_tensor(f"o_sb{i}", [128, D], F32))
                    for i in range(4)]
            mvbB = es4B.enter_context(nc.sbuf_tensor("mvbB", [128, 2 * TC], F32))
            rsbB = es4B.enter_context(nc.sbuf_tensor("rsbB", [128, TC], F32))
            nmbB = es4B.enter_context(nc.sbuf_tensor("nmbB", [128, TC], F32))

            for Q in (SY, SC, G):
                Q.wait_ge(s_aevB.h, s_aevB.v)
                for i in range(8):
                    Q.wait_ge(s_stgB[i].h, s_stgB[i].v)
            for tc in range(TC):
                V.bn_aggr(mvbB[:, 2 * tc:2 * tc + 2], estats[:, tc * 48:(tc + 1) * 48])
                V.drain()
            mview = mvbB[:, :].rearrange("p (t two) -> p t two", two=2)
            s_v3B.inc(V.tensor_scalar(nmbB[:, :], mview[:, :, 0:1].squeeze(2),
                                      -1.0, None, OP.mult))
            A.wait_ge(s_v3B.h, s_v3B.v)
            A.activation(rsbB[:, :], mview[:, :, 1:2].squeeze(2),
                         AF.Ln, bias=eps_t[:, 0:1], scale=1.0)
            A.drain()
            s_a3B.inc(A.activation(rsbB[:, :], rsbB[:, :], AF.Exp, scale=-0.5))
            V.wait_ge(s_a3B.h, s_a3B.v)
            s_v3B.inc(V.tensor_tensor(nmbB[:, :], nmbB[:, :], rsbB[:, :], OP.mult))
            V.drain()
            s_e = s_peB[0]
            s_xq = {SC: s_pxxB[0], G: s_pxxB[1], SY: s_p3B}
            s_oq = {G: s_outB, SC: s_peB[1]}
            cmp_mark = [0, 0, 0, 0]
            st_mark = [None] * 4
            for tc in range(TC):
                sl = tc % 4
                XQ = (SC, G, SY)[tc % 3]
                OQ = G if tc % 2 == 0 else SC
                if tc >= 4:
                    SY.wait_ge(s_v3B.h, cmp_mark[sl])
                s_e.inc(SY.dma_start(
                    out=ebuf[sl][:, :],
                    in_=e_dram.ap()[tc * 128:(tc + 1) * 128, :]), 16)
                if tc >= 4:
                    XQ.wait_ge(s_v3B.h, cmp_mark[sl])
                sx = s_xq[XQ]
                sx.inc(XQ.dma_start(
                    out=x2b[sl][:, :], in_=x2_d[tc * 128:(tc + 1) * 128, :]), 16)
                V.wait_ge(sx.h, sx.v)
                V.tensor_scalar(x2b[sl][:, :], x2b[sl][:, :],
                                nmbB[:, tc:tc + 1], None, OP.add)
                V.drain()
                V.wait_ge(s_e.h, s_e.v)
                if tc >= 4:
                    V.wait_ge(st_mark[sl][0].h, st_mark[sl][1])
                s_v3B.inc(V.scalar_tensor_tensor(
                    o_sb[sl][:, :], ebuf[sl][:, :], rsbB[:, tc:tc + 1],
                    x2b[sl][:, :], OP.mult, OP.add))
                cmp_mark[sl] = s_v3B.v
                OQ.wait_ge(s_v3B.h, cmp_mark[sl])
                so = s_oq[OQ]
                so.inc(OQ.dma_start(
                    out=out_d[tc * 128:(tc + 1) * 128, :],
                    in_=o_sb[sl][:, :]), 16)
                st_mark[sl] = (so, so.v)
            SC.wait_ge(s_oq[G].h, s_oq[G].v)
            SC.wait_ge(s_oq[SC].h, s_oq[SC].v)
        es_bflat.close()

    return nc



def kernel(L0_attn_data, L0_attn_weight, L0_attn_mask, L0_ffn_weight,
           L0_ffn_lora_A, L0_ffn_lora_B, L0_ffn_gw1, L0_ffn_gb1,
           L0_ffn_ln_gamma, L0_ffn_ln_beta, L0_ffn_gw2, L0_ffn_gb2):
    x = np.asarray(L0_attn_data, dtype=np.float32)
    waT = np.ascontiguousarray(np.asarray(L0_attn_weight, np.float32).T)
    wfT = np.ascontiguousarray(np.asarray(L0_ffn_weight, np.float32).T)
    mask = np.asarray(L0_attn_mask, np.int32)
    maskR = np.ascontiguousarray(
        mask.reshape(128, 32, 32, 4, 32).transpose(3, 2, 0, 4, 1).reshape(128, 128, 1024))
    aRe = np.ascontiguousarray(
        np.asarray(L0_ffn_lora_A, np.float32).transpose(1, 0, 2).reshape(D, 128))
    bRe = np.ascontiguousarray(np.asarray(L0_ffn_lora_B, np.float32).reshape(128, D))
    gw1T = np.ascontiguousarray(np.asarray(L0_ffn_gw1, np.float32).T)
    gw2T = np.ascontiguousarray(np.tile(np.asarray(L0_ffn_gw2, np.float32).T, (4, 1)))
    gb1B = np.ascontiguousarray(np.broadcast_to(
        np.asarray(L0_ffn_gb1, np.float32), (128, 32)))
    gamB = np.ascontiguousarray(np.broadcast_to(
        np.asarray(L0_ffn_ln_gamma, np.float32), (128, 32)))
    betB = np.ascontiguousarray(np.broadcast_to(
        np.asarray(L0_ffn_ln_beta, np.float32), (128, 32)))
    gb2B = np.ascontiguousarray(np.broadcast_to(
        np.asarray(L0_ffn_gb2, np.float32), (128, 8)))

    nc = build()
    in_maps = [{
        "x": np.ascontiguousarray(x[c]),
        "waT": waT, "maskR": np.ascontiguousarray(maskR[:, 16 * c:16 * (c + 1), :]),
        "aRe": aRe,
        "gw1T": gw1T, "gw2T": gw2T,
        "gb1B": gb1B, "gamB": gamB, "betB": betB, "gb2B": gb2B,
        "wfT": wfT, "bRe": bRe,
    } for c in range(NCORES)]
    res = run_bass_kernel_spmd(nc, in_maps, core_ids=list(range(NCORES)))

    out = np.stack([res.results[c]["out"] for c in range(NCORES)], axis=0)
    return np.ascontiguousarray(out, dtype=np.float32)


# version 21
# speedup vs baseline: 1.0153x; 1.0153x over previous
import numpy as np
from contextlib import ExitStack

import concourse.bass as bass
import concourse.mybir as mybir
from concourse.bass_utils import run_bass_kernel_spmd
from concourse.alu_op_type import AluOpType as OP

F32 = mybir.dt.float32
BF16 = mybir.dt.bfloat16
I32 = mybir.dt.int32
AF = mybir.ActivationFunctionType
AX = mybir.AxisListType.X

NCORES = 8
TOK = 2048
TC = 16
D = 4096
KC = 32
KH = 16
NTN = 8
NTW = 512
TG = 4
NWS = 3
EPS = 1e-5
CAPACITY = float(int(3.0 * (NCORES * TOK) / 8))
BIG_NEG = -1.0e30
ALPHA = 1.0 / 16.0


class Sem:

    def __init__(self, es, nc, name):
        self.h = es.enter_context(nc.semaphore(name))
        self.v = 0

    def inc(self, instr, k=1):
        instr.then_inc(self.h, k)
        self.v += k
        return instr


def ln_chain(V, A, x_ap, out_ap, st, mv, nm, nmr, rstd, eps_t, sv, sa, nstats,
             APL=None):
    APL = APL if APL is not None else V
    w = D // nstats
    for i in range(nstats):
        V.bn_stats(st[:, 6 * i:6 * i + 6], x_ap[:, w * i:w * (i + 1)])
    V.drain()
    V.bn_aggr(mv[:, :], st[:, 0:6 * nstats])
    V.drain()
    sv.inc(V.tensor_scalar(nm[:, :], mv[:, 0:1], -1.0, None, OP.mult))
    A.wait_ge(sv.h, sv.v)
    A.activation(rstd[:, :], mv[:, 1:2], AF.Ln, bias=eps_t[:, 0:1], scale=1.0)
    A.drain()
    sa.inc(A.activation(rstd[:, :], rstd[:, :], AF.Exp, scale=-0.5))
    APL.wait_ge(sa.h, sa.v)
    if APL is not V:
        APL.wait_ge(sv.h, sv.v)
    APL.tensor_tensor(nmr[:, :], nm[:, :], rstd[:, :], OP.mult)
    if APL is V:
        APL.drain()
    return APL.tensor_scalar(out_ap, x_ap, rstd[:, 0:1], nmr[:, 0:1], OP.mult, OP.add)



def build():
    nc = bass.Bass("TRN2", target_bir_lowering=False, debug=False)

    x_d = nc.dram_tensor("x", [TOK, D], F32, kind="ExternalInput").ap()
    waT_d = nc.dram_tensor("waT", [D, D], F32, kind="ExternalInput").ap()
    maskR_d = nc.dram_tensor("maskR", [128, 16, 1024], I32, kind="ExternalInput").ap()
    aRe_d = nc.dram_tensor("aRe", [D, 128], F32, kind="ExternalInput").ap()
    gw1T_d = nc.dram_tensor("gw1T", [D, 32], F32, kind="ExternalInput").ap()
    gw2T_d = nc.dram_tensor("gw2T", [128, 8], F32, kind="ExternalInput").ap()
    gb1B_d = nc.dram_tensor("gb1B", [128, 32], F32, kind="ExternalInput").ap()
    gamB_d = nc.dram_tensor("gamB", [128, 32], F32, kind="ExternalInput").ap()
    betB_d = nc.dram_tensor("betB", [128, 32], F32, kind="ExternalInput").ap()
    gb2B_d = nc.dram_tensor("gb2B", [128, 8], F32, kind="ExternalInput").ap()
    wfT_d = nc.dram_tensor("wfT", [D, D], F32, kind="ExternalInput").ap()
    bRe_d = nc.dram_tensor("bRe", [128, D], F32, kind="ExternalInput").ap()

    out_d = nc.dram_tensor("out", [TOK, D], F32, kind="ExternalOutput").ap()

    b_dram = nc.dram_tensor("b_scr", [TOK, D], BF16)
    bin_dram = nc.dram_tensor("bin_scr", [128, 128], BF16)
    x2_d = nc.dram_tensor("x2_scr", [TOK, D], F32).ap()
    l2t_d = nc.dram_tensor("l2t_scr", [TOK, D], BF16).ap()
    e_dram = nc.dram_tensor("e_scr", [TOK, D], BF16)
    cc_in = nc.dram_tensor("cc_in", [1, 16], F32)
    cc_out = nc.dram_tensor("cc_out", [1, 16], F32)
    awd_dram = nc.dram_tensor("awd_scr", [1, 256], F32)
    ag_in = nc.dram_tensor("ag_in", [128, 16], F32)
    ag_out = nc.dram_tensor("ag_out", [1024, 16], F32)

    PE, V, A, G, SY, SC = nc.tensor, nc.vector, nc.scalar, nc.gpsimd, nc.sync, nc.scalar

    with ExitStack() as es:
        e = es.enter_context

        ident = e(nc.sbuf_tensor("ident", [128, 128], BF16))
        e4t = e(nc.sbuf_tensor("e4t", [4, 128], BF16))
        eps_t = e(nc.sbuf_tensor("eps_t", [128, 1], F32))
        ones_b = e(nc.sbuf_tensor("ones_b", [128, 1], BF16))
        gw2_sb = e(nc.sbuf_tensor("gw2_sb", [128, 8], BF16))
        gb1B = e(nc.sbuf_tensor("gb1B_s", [128, 32], F32))
        gamB = e(nc.sbuf_tensor("gamB_s", [128, 32], F32))
        betB = e(nc.sbuf_tensor("betB_s", [128, 32], F32))
        gb2B = e(nc.sbuf_tensor("gb2B_s", [128, 8], F32))
        stats = [e(nc.sbuf_tensor(f"st{i}", [128, 96], F32)) for i in range(2)]
        mv = [e(nc.sbuf_tensor(f"mv{i}", [128, 2], F32)) for i in range(2)]
        nm = [e(nc.sbuf_tensor(f"nm{i}", [128, 1], F32)) for i in range(2)]
        nmr = [e(nc.sbuf_tensor(f"nmr{i}", [128, 1], F32)) for i in range(2)]
        rstd = [e(nc.sbuf_tensor(f"rs{i}", [128, 1], F32)) for i in range(2)]

        s_const = Sem(es, nc, "s_const")
        s_ld1 = Sem(es, nc, "s_ld1")
        s_msk = [Sem(es, nc, f"s_msk{i}") for i in range(2)]
        s_bin = Sem(es, nc, "s_bin")
        s_binp = Sem(es, nc, "s_binp")
        s_binpm = Sem(es, nc, "s_binpm")
        s_ldx = [Sem(es, nc, f"s_ldx{i}") for i in range(2)]
        s_abf = [Sem(es, nc, f"s_abf{i}") for i in range(2)]
        s_aused = Sem(es, nc, "s_aused")
        s_lnv = Sem(es, nc, "s_lnv")
        s_lna = Sem(es, nc, "s_lna")
        s_pt = Sem(es, nc, "s_pt")
        s_ev = Sem(es, nc, "s_ev")
        s_wm = [Sem(es, nc, f"s_wm{i}") for i in range(NWS)]
        s_wmf = [Sem(es, nc, f"s_wmf{i}") for i in range(NWS)]
        s_wa = [Sem(es, nc, f"s_wa{i}") for i in range(NWS)]
        s_mm = Sem(es, nc, "s_mm")
        s_aev = Sem(es, nc, "s_aev")
        s_stg = [Sem(es, nc, f"s_stg{i}") for i in range(8)]
        s_pb = [Sem(es, nc, f"s_pb{i}") for i in range(2)]
        s_px = [Sem(es, nc, f"s_px{i}") for i in range(2)]
        s_v4 = Sem(es, nc, "s_v4")
        s_a4 = Sem(es, nc, "s_a4")
        s_p4 = Sem(es, nc, "s_p4")
        s_pe4 = Sem(es, nc, "s_pe4")
        s_tp4 = Sem(es, nc, "s_tp4")
        s_l2u = Sem(es, nc, "s_l2u")
        s_x2s = [Sem(es, nc, f"s_x2s{i}") for i in range(2)]
        s_l2 = [Sem(es, nc, f"s_l2{i}") for i in range(2)]
        s_cci = Sem(es, nc, "s_cci")
        s_cco = Sem(es, nc, "s_cco")
        s_ccd = e(nc.semaphore("s_ccd"))
        s_bsi = Sem(es, nc, "s_bsi")
        s_bsd = Sem(es, nc, "s_bsd")
        s_ccd2 = e(nc.semaphore("s_ccd2"))
        s_q1 = Sem(es, nc, "s_q1")
        s_awx = Sem(es, nc, "s_awx")
        s_l2B = Sem(es, nc, "s_l2B")
        s_l2C = Sem(es, nc, "s_l2C")
        s_wfB = [Sem(es, nc, f"s_wfB{i}") for i in range(NWS)]
        s_wffB = [Sem(es, nc, f"s_wffB{i}") for i in range(NWS)]
        s_mmB = Sem(es, nc, "s_mmB")
        s_aevB = Sem(es, nc, "s_aevB")
        s_stgB = [Sem(es, nc, f"s_stgB{i}") for i in range(8)]
        s_peB = [Sem(es, nc, f"s_peB{i}") for i in range(2)]
        s_pxxB = [Sem(es, nc, f"s_pxxB{i}") for i in range(2)]
        s_v3B = Sem(es, nc, "s_v3B")
        s_a3B = Sem(es, nc, "s_a3B")
        s_p3B = Sem(es, nc, "s_p3B")
        s_outB = Sem(es, nc, "s_outB")

        blk = e(nc.Block())

        G.memset(ident[:, :], 0.0)
        G.affine_select(out=ident[:, :], in_=ident[:, :], compare_op=OP.not_equal,
                        fill=1.0, base=0, pattern=[[-1, 128]], channel_multiplier=1)
        G.memset(e4t[:, :], 1.0)
        G.affine_select(out=e4t[:, :], in_=e4t[:, :], compare_op=OP.is_ge,
                        fill=0.0, base=0, pattern=[[1, 128]], channel_multiplier=-32)
        G.affine_select(out=e4t[:, :], in_=e4t[:, :], compare_op=OP.is_ge,
                        fill=0.0, base=31, pattern=[[-1, 128]], channel_multiplier=32)
        s_const.inc(G.memset(ones_b[:, :], 1.0))
        s_const.inc(G.dma_start(out=gw2_sb[:, :], in_=gw2T_d), 16)
        s_ld1.inc(SY.dma_start(out=gb1B[:, :], in_=gb1B_d), 16)
        s_ld1.inc(SY.dma_start(out=gamB[:, :], in_=gamB_d), 16)
        s_ld1.inc(SY.dma_start(out=betB[:, :], in_=betB_d), 16)
        s_ld1.inc(SY.dma_start(out=gb2B[:, :], in_=gb2B_d), 16)
        V.memset(eps_t[:, :], EPS)
        V.drain()

        es_big = ExitStack()
        aT = es_big.enter_context(nc.sbuf_tensor("aT", [128, KC * TOK], BF16))
        binP = es_big.enter_context(nc.sbuf_tensor("binP", [128, 4096], BF16))
        bs_part = es_big.enter_context(nc.sbuf_tensor("bs_part", [128, 16], F32))
        bs_all = es_big.enter_context(nc.sbuf_tensor("bs_all", [128, 128], F32))
        binX = es_big.enter_context(nc.sbuf_tensor("binX", [128, 128], BF16))
        es_bst = ExitStack()

        es_msk = ExitStack()
        msk = [es_msk.enter_context(nc.sbuf_tensor(f"msk{i}", [128, 4096], BF16))
               for i in range(2)]

        with ExitStack() as es2:
            xbuf = [es2.enter_context(nc.sbuf_tensor(f"xb{i}", [128, D], F32))
                    for i in range(2)]
            abf = [es2.enter_context(nc.sbuf_tensor(f"ab{i}", [128, D], BF16))
                   for i in range(2)]
            ps_tp = [es2.enter_context(nc.psum_tensor(f"ps_tp{i}", [128, 1024], BF16))
                     for i in range(2)]

            aTv = aT[:, :].rearrange("p (kc t) -> p kc t", kc=KC)
            for tc in range(TC):
                sl = tc % 2
                XQ2 = SY if (tc < 8 or tc % 2 == 0) else G
                if tc >= 2:
                    XQ2.wait_ge(s_abf[sl].h, s_abf[sl].v)
                s_ldx[sl].inc(XQ2.dma_start(
                    out=xbuf[sl][:, :], in_=x_d[tc * 128:(tc + 1) * 128, :]), 16)
                V.wait_ge(s_ldx[sl].h, s_ldx[sl].v)
                if tc >= 2:
                    V.wait_ge(s_aused.h, tc - 1)
                ap = ln_chain(V, A, xbuf[sl][:, :], abf[sl][:, :], stats[sl], mv[sl],
                              nm[sl], nmr[sl], rstd[sl], eps_t, s_lnv, s_lna, 8)
                s_abf[sl].inc(ap)
                if tc == 0:
                    PE.wait_ge(s_const.h, s_const.v)
                PE.wait_ge(s_abf[sl].h, s_abf[sl].v)
                for kc in range(KC):
                    j = tc * KC + kc
                    bank = (j // 8) % 2
                    slot = j % 8
                    if j >= 16 and slot == 0:
                        PE.wait_ge(s_ev.h, j // 8 - 1)
                    s_pt.inc(PE.transpose(ps_tp[bank][:, slot * 128:(slot + 1) * 128],
                                          abf[sl][:, kc * 128:(kc + 1) * 128],
                                          ident[:, :]))
                    if kc == KC - 1:
                        s_aused.inc(PE.nop())
                    if slot == 7:
                        A.wait_ge(s_pt.h, s_pt.v)
                        s_ev.inc(A.copy(aTv[:, kc - 7:kc + 1, tc * 128:(tc + 1) * 128],
                                        ps_tp[bank][:, :].rearrange(
                                            "p (s t) -> p s t", s=8)))
                if tc < 4:
                    msl = tc % 2
                    if tc >= 2:
                        G.wait_ge(s_msk[msl].h, s_msk[msl].v)
                    s_msk[msl].inc(G.dma_start(
                        out=msk[msl][:, :].rearrange("p (a j) -> p a j", a=4),
                        in_=maskR_d[:, 4 * tc:4 * (tc + 1), :]), 16)
                    V.wait_ge(s_msk[msl].h, s_msk[msl].v)
                    s_msk[msl].inc(V.tensor_reduce(
                        bs_part[:, 4 * tc:4 * (tc + 1)],
                        msk[msl][:, :].rearrange("p (a j) -> p a j", a=4),
                        axis=AX, op=OP.add))
                    V.drain()
                if tc == 3:
                    SC.wait_ge(s_msk[0].h, s_msk[0].v)
                    SC.wait_ge(s_msk[1].h, s_msk[1].v)
                    s_bsi.inc(SC.dma_start(out=ag_in.ap(), in_=bs_part[:, :]), 16)
                if tc == 4:
                    G.wait_ge(s_bsi.h, s_bsi.v)
                    G.collective_compute(
                        "AllGather", OP.bypass,
                        replica_groups=[list(range(NCORES))],
                        ins=[ag_in.ap().opt()], outs=[ag_out.ap().opt()],
                    ).then_inc(s_ccd2)
                if tc == 6:
                    SC.wait_ge(s_ccd2, 1)
                    s_bsd.inc(SC.dma_start(
                        out=bs_all[:, :].rearrange("p (r j) -> p r j", r=8),
                        in_=ag_out.ap().rearrange("(r p) j -> p r j", r=8)), 16)

        V.wait_ge(s_bsd.h, s_bsd.v)
        s_bin.inc(V.tensor_scalar(binX[:, :], bs_all[:, :], 0.0, None, OP.is_gt))
        SC.wait_ge(s_bin.h, s_bin.v)
        s_bin.inc(SC.dma_start(out=bin_dram.ap(), in_=binX[:, :]), 16)
        with ExitStack() as esb:
            bin_g = esb.enter_context(nc.sbuf_tensor("bin_g", [4, 4096], BF16))
            ps_bin = [esb.enter_context(nc.psum_tensor(f"ps_bin{i}", [128, 512], F32))
                      for i in range(2)]
            SY.wait_ge(s_bin.h, s_bin.v)
            s_bin.inc(SY.dma_start(
                out=bin_g[:, :].rearrange("g (kc nb) -> g kc nb", kc=KC),
                in_=bin_dram.ap().rearrange("(g kc) nb -> g kc nb", g=4)), 16)
            PE.wait_ge(s_bin.h, s_bin.v)
            bgv = bin_g[:, :].rearrange("g (kc nb) -> g kc nb", kc=KC)
            for nt in range(NTN):
                if nt >= 2:
                    PE.wait_ge(s_binp.h, nt - 1)
                s_binpm.inc(PE.matmul(ps_bin[nt % 2][:, 0:512], e4t[:, :],
                                      bgv[:, :, nt * 16:(nt + 1) * 16],
                                      start=True, stop=True))
                A.wait_ge(s_binpm.h, s_binpm.v)
                s_binp.inc(A.copy(binP[:, nt * 512:(nt + 1) * 512],
                                  ps_bin[nt % 2][:, 0:512]))
        es_msk.close()

        es_t14 = ExitStack()
        t1_sb = es_t14.enter_context(nc.sbuf_tensor("t1_sb", [128, TOK], BF16,
                                                    side="right"))
        t2_sb = es_t14.enter_context(nc.sbuf_tensor("t2_sb", [128, TOK], BF16,
                                                    side="right"))
        bstats = es_bst.enter_context(
            nc.sbuf_tensor("bstats", [128, TC * NTN * 6], F32, side="right"))
        with ExitStack() as es3:
            wm = [es3.enter_context(nc.sbuf_tensor(f"wm{i}", [128, KH * NTW], BF16))
                  for i in range(NWS)]
            stage = [es3.enter_context(nc.sbuf_tensor(f"stg{i}", [128, NTW], BF16))
                     for i in range(8)]
            ps_acc = [es3.enter_context(nc.psum_tensor(f"ps_acc{i}", [128, 512], F32))
                      for i in range(8)]

            grp = 0
            for nt in range(NTN):
                for kh in range(2):
                    sidx = nt * 2 + kh
                    wsl = sidx % NWS
                    if sidx == 0:
                        G.wait_ge(s_binp.h, s_binp.v)
                        G.wait_ge(s_msk[0].h, s_msk[0].v)
                        G.wait_ge(s_msk[1].h, s_msk[1].v)
                    if sidx >= NWS:
                        G.wait_ge(s_wmf[wsl].h, s_wmf[wsl].v)
                    s_wa[wsl].inc(G.dma_start(
                        out=wm[wsl][:, :].rearrange("p (kc n) -> p kc n", kc=KH),
                        in_=waT_d[kh * KH * 128:(kh + 1) * KH * 128,
                                  nt * NTW:(nt + 1) * NTW]
                            .rearrange("(kc kp) n -> kp kc n", kp=128)), 16)
                    APL = G
                    APL.wait_ge(s_wa[wsl].h, s_wa[wsl].v)
                    APL.wait_ge(s_binp.h, s_binp.v)
                    bview = (binP[:, nt * 512 + kh * 256:nt * 512 + (kh + 1) * 256]
                             .rearrange("p (kc nb) -> p kc nb", kc=KH)
                             .unsqueeze(3).broadcast_to([128, KH, 16, 32]))
                    wview = wm[wsl][:, :].rearrange("p (kc nb n) -> p kc nb n",
                                                    kc=KH, nb=16)
                    s_wm[wsl].inc(APL.tensor_tensor(wview, wview, bview, OP.mult))

                for tg in range(TG):
                    bank = (grp % 2) * 4
                    if grp == 0:
                        PE.wait_ge(s_ev.h, s_ev.v)
                        PE.wait_ge(s_binp.h, s_binp.v)
                    if grp >= 2:
                        PE.wait_ge(s_aev.h, 4 * (grp - 1))
                    mmi = None
                    for kc in range(KC):
                        wsl = (nt * 2 + kc // KH) % NWS
                        if kc % KH == 0:
                            PE.wait_ge(s_wm[wsl].h, s_wm[wsl].v)
                        for tci in range(4):
                            tc = tg * 4 + tci
                            mmi = PE.matmul(
                                ps_acc[bank + tci][:, 0:NTW],
                                aT[:, kc * TOK + tc * 128:kc * TOK + (tc + 1) * 128],
                                wm[wsl][:, (kc % KH) * NTW:(kc % KH + 1) * NTW],
                                start=(kc == 0), stop=(kc == KC - 1))
                    s_mm.inc(mmi)
                    if tg == TG - 1:
                        s_wmf[(nt * 2) % NWS].inc(PE.nop())
                        s_wmf[(nt * 2 + 1) % NWS].inc(PE.nop())
                    A.wait_ge(s_mm.h, s_mm.v)
                    for tci in range(4):
                        tc = tg * 4 + tci
                        ssl = (grp % 2) * 4 + tci
                        uses = grp // 2
                        if uses >= 1:
                            A.wait_ge(s_stg[ssl].h, 17 * uses)
                        s_aev.inc(A.copy(stage[ssl][:, :], ps_acc[bank + tci][:, 0:NTW]))
                        SC.wait_ge(s_aev.h, s_aev.v)
                        s_stg[ssl].inc(SC.dma_start(
                            out=b_dram.ap()[tc * 128:(tc + 1) * 128,
                                            nt * NTW:(nt + 1) * NTW],
                            in_=stage[ssl][:, :]), 16)
                        V.wait_ge(s_aev.h, s_aev.v)
                        s_stg[ssl].inc(V.bn_stats(
                            bstats[:, (tc * NTN + nt) * 6:(tc * NTN + nt) * 6 + 6],
                            stage[ssl][:, :]))
                    grp += 1
        es_big.close()

        with ExitStack() as es4:
            gw1_sb = es4.enter_context(nc.sbuf_tensor("gw1_sb", [128, KC * 32], BF16))
            A_sb = es4.enter_context(nc.sbuf_tensor("A_sb", [128, KC * 128], BF16))
            mvb = es4.enter_context(nc.sbuf_tensor("mvb", [128, 2 * TC], F32))
            rsb = es4.enter_context(nc.sbuf_tensor("rsb", [128, TC], F32))
            nmb = es4.enter_context(nc.sbuf_tensor("nmb", [128, TC], F32))
            h_all = es4.enter_context(nc.sbuf_tensor("h_all", [128, TC * 32], F32))
            hsq = es4.enter_context(nc.sbuf_tensor("hsq", [128, TC * 32], F32))
            hmu = es4.enter_context(nc.sbuf_tensor("hmu", [128, TC], F32))
            hvar = es4.enter_context(nc.sbuf_tensor("hvar", [128, TC], F32))
            hrs = es4.enter_context(nc.sbuf_tensor("hrs", [128, TC], F32))
            hb_all = es4.enter_context(nc.sbuf_tensor("hb_all", [128, TC * 32], BF16))
            hT_all = es4.enter_context(nc.sbuf_tensor("hT_all", [32, TC * 128], BF16))
            gat_all = es4.enter_context(nc.sbuf_tensor("gat_all", [128, TC * 8], F32))
            gat2_all = es4.enter_context(nc.sbuf_tensor("gat2_all", [128, TC * 8], F32))
            m1a = es4.enter_context(nc.sbuf_tensor("m1a", [128, TC], F32))
            m2a = es4.enter_context(nc.sbuf_tensor("m2a", [128, TC], F32))
            dga = es4.enter_context(nc.sbuf_tensor("dga", [128, TC], F32))
            sxa = es4.enter_context(nc.sbuf_tensor("sxa", [128, TC], F32))
            spa = es4.enter_context(nc.sbuf_tensor("spa", [128, TC], F32))
            rsa = es4.enter_context(nc.sbuf_tensor("rsa", [128, TC], F32))
            p1a = es4.enter_context(nc.sbuf_tensor("p1a", [128, TC], F32))
            p2a = es4.enter_context(nc.sbuf_tensor("p2a", [128, TC], F32))
            eq1a = es4.enter_context(nc.sbuf_tensor("eq1a", [128, TC * 8], F32))
            eq2a = es4.enter_context(nc.sbuf_tensor("eq2a", [128, TC * 8], F32))
            g1a = es4.enter_context(nc.sbuf_tensor("g1a", [128, TC * 8], F32))
            g2a = es4.enter_context(nc.sbuf_tensor("g2a", [128, TC * 8], F32))
            eqb_all = es4.enter_context(nc.sbuf_tensor("eqb_all", [128, TC * 16], BF16))
            tmpf_all = es4.enter_context(nc.sbuf_tensor("tmpf_all", [128, TC * 128], F32))
            t1b_all = es4.enter_context(nc.sbuf_tensor("t1b_all", [128, TC * 128], BF16))
            t2b_all = es4.enter_context(nc.sbuf_tensor("t2b_all", [128, TC * 128], BF16))
            cntf = es4.enter_context(nc.sbuf_tensor("cntf", [1, 256], F32))
            cnt_row = es4.enter_context(nc.sbuf_tensor("cnt_row", [1, 16], F32))
            ps_tp4 = [es4.enter_context(nc.psum_tensor(f"ps_tp4{i}", [128, 1024], BF16))
                      for i in range(2)]
            ps_h = es4.enter_context(nc.psum_tensor("ps_h", [128, 512], F32))
            ps_tmp = es4.enter_context(nc.psum_tensor("ps_tmp", [128, 512], F32))
            ps_hT4 = es4.enter_context(nc.psum_tensor("ps_hT4", [32, TC * 128], BF16))
            ps_g = es4.enter_context(nc.psum_tensor("ps_g", [128, 128], F32))
            ps_cnt2 = es4.enter_context(nc.psum_tensor("ps_cnt2", [1, 256], F32))
            bbuf = [es4.enter_context(nc.sbuf_tensor(f"bb{i}", [128, D], BF16))
                    for i in range(2)]
            xbuf4 = [es4.enter_context(nc.sbuf_tensor(f"xb4{i}", [128, D], F32))
                     for i in range(2)]
            x2_sb = [es4.enter_context(nc.sbuf_tensor(f"x2_{i}", [128, D], F32))
                     for i in range(2)]
            l2bf = [es4.enter_context(nc.sbuf_tensor(f"l2b{i}", [128, D], BF16))
                    for i in range(2)]
            l2t = [es4.enter_context(nc.sbuf_tensor(f"l2t{i}", [128, D], BF16))
                   for i in range(2)]

            G.wait_ge(s_aev.h, s_aev.v)
            s_const.inc(G.dma_start(
                out=gw1_sb[:, :].rearrange("p (kc g) -> p kc g", kc=KC),
                in_=gw1T_d.rearrange("(kc kp) g -> kp kc g", kp=128)), 16)
            s_const.inc(G.dma_start(
                out=A_sb[:, :].rearrange("p (kc er) -> p kc er", kc=KC),
                in_=aRe_d.rearrange("(kc kp) er -> kp kc er", kp=128)), 16)

            SY.wait_ge(s_aev.h, s_aev.v)
            for i in range(8):
                SY.wait_ge(s_stg[i].h, s_stg[i].v)

            for tc in range(TC):
                V.bn_aggr(mvb[:, 2 * tc:2 * tc + 2], bstats[:, tc * 48:(tc + 1) * 48])
                V.drain()
            s_v4.inc(V.tensor_scalar(nmb[:, :],
                                     mvb[:, :].rearrange("p (t two) -> p t two", two=2)[:, :, 0:1].squeeze(2),
                                     -1.0, None, OP.mult))
            A.wait_ge(s_v4.h, s_v4.v)
            A.activation(rsb[:, :],
                         mvb[:, :].rearrange("p (t two) -> p t two", two=2)[:, :, 1:2].squeeze(2),
                         AF.Ln, bias=eps_t[:, 0:1], scale=1.0)
            A.drain()
            s_a4.inc(A.activation(rsb[:, :], rsb[:, :], AF.Exp, scale=-0.5))
            V.wait_ge(s_a4.h, s_a4.v)
            s_v4.inc(V.tensor_tensor(nmb[:, :], nmb[:, :], rsb[:, :], OP.mult))
            V.drain()
            es_bst.close()

            a4_gate_prev = 0
            ptc = 0
            stt_v4 = [0, 0]
            ln_a4 = [0, 0]
            for step in range(TC + 1):
                if step < TC:
                    tc = step
                    sl = tc % 2
                    if tc >= 1:
                        psl = (tc - 1) % 2
                        SQ4 = SY if psl == 0 else SC
                        SQ4.wait_ge(s_v4.h, stt_v4[psl])
                        s_x2s[psl].inc(SQ4.dma_start(
                            out=x2_d[(tc - 1) * 128:tc * 128, :],
                            in_=x2_sb[psl][:, :]), 16)
                    if tc >= 2:
                        SY.wait_ge(s_v4.h, stt_v4[sl])
                        G.wait_ge(s_v4.h, stt_v4[sl])
                    s_pb[sl].inc(SY.dma_start(
                        out=bbuf[sl][:, :],
                        in_=b_dram.ap()[tc * 128:(tc + 1) * 128, :]), 16)
                    s_px[sl].inc(G.dma_start(
                        out=xbuf4[sl][:, :], in_=x_d[tc * 128:(tc + 1) * 128, :]), 16)
                    V.wait_ge(s_pb[sl].h, s_pb[sl].v)
                    V.wait_ge(s_px[sl].h, s_px[sl].v)
                    if tc >= 2:
                        V.wait_ge(s_x2s[sl].h, s_x2s[sl].v)
                    V.tensor_scalar(xbuf4[sl][:, :], xbuf4[sl][:, :],
                                    nmb[:, tc:tc + 1], None, OP.add)
                    V.drain()
                    s_v4.inc(V.scalar_tensor_tensor(
                        x2_sb[sl][:, :], bbuf[sl][:, :], rsb[:, tc:tc + 1],
                        xbuf4[sl][:, :], OP.mult, OP.add))
                    stt_v4[sl] = s_v4.v
                    for i in range(8):
                        V.bn_stats(stats[sl][:, 6 * i:6 * i + 6],
                                   x2_sb[sl][:, 512 * i:512 * (i + 1)])
                    V.drain()
                    V.bn_aggr(mv[sl][:, :], stats[sl][:, 0:48])
                    V.drain()
                    s_v4.inc(V.tensor_scalar(nm[sl][:, :], mv[sl][:, 0:1], -1.0,
                                             None, OP.mult))
                    A.wait_ge(s_v4.h, s_v4.v)
                    A.activation(rstd[sl][:, :], mv[sl][:, 1:2], AF.Ln,
                                 bias=eps_t[:, 0:1], scale=1.0)
                    A.drain()
                    s_a4.inc(A.activation(rstd[sl][:, :], rstd[sl][:, :], AF.Exp,
                                          scale=-0.5))
                    ln_a4[sl] = s_a4.v
                if step < 1:
                    continue
                tc = step - 1
                sl = tc % 2
                V.wait_ge(s_a4.h, ln_a4[sl])
                if tc >= 2:
                    V.wait_ge(s_l2u.h, tc - 1)
                V.tensor_tensor(nmr[sl][:, :], nm[sl][:, :], rstd[sl][:, :], OP.mult)
                V.drain()
                s_v4.inc(V.tensor_scalar(l2bf[sl][:, :], x2_sb[sl][:, :],
                                         rstd[sl][:, 0:1], nmr[sl][:, 0:1],
                                         OP.mult, OP.add))

                PE.wait_ge(s_v4.h, s_v4.v)
                if tc >= 2:
                    PE.wait_ge(s_l2[sl].h, s_l2[sl].v)
                for kc in range(KC):
                    bank = (ptc // 8) % 2
                    slot = ptc % 8
                    if slot == 0 and ptc >= 16:
                        PE.wait_ge(s_tp4.h, ptc // 8 - 1)
                    s_pe4.inc(PE.transpose(
                        ps_tp4[bank][:, slot * 128:(slot + 1) * 128],
                        l2bf[sl][:, kc * 128:(kc + 1) * 128], ident[:, :]))
                    if kc == KC - 1:
                        s_l2u.inc(PE.nop())
                    if slot == 7:
                        A.wait_ge(s_pe4.h, s_pe4.v)
                        s_tp4.inc(A.copy(
                            l2t[sl][:, (kc - 7) * 128:(kc + 1) * 128],
                            ps_tp4[bank][:, 0:1024]))
                    ptc += 1
                SC.wait_ge(s_tp4.h, s_tp4.v)
                s_l2[sl].inc(SC.dma_start(
                    out=l2t_d[tc * 128:(tc + 1) * 128, :], in_=l2t[sl][:, :]), 16)

                PE.wait_ge(s_tp4.h, s_tp4.v)
                PE.wait_ge(s_a4.h, a4_gate_prev)
                PE.wait_ge(s_const.h, s_const.v)
                mmi = None
                for kc in range(KC):
                    blkap = l2t[sl][:, kc * 128:(kc + 1) * 128]
                    PE.matmul(ps_h[:, 0:32], blkap, gw1_sb[:, kc * 32:(kc + 1) * 32],
                              start=(kc == 0), stop=(kc == KC - 1))
                    mmi = PE.matmul(ps_tmp[:, 0:128], blkap,
                                    A_sb[:, kc * 128:(kc + 1) * 128],
                                    start=(kc == 0), stop=(kc == KC - 1))
                s_pe4.inc(mmi)
                s_l2[sl].inc(PE.nop())
                A.wait_ge(s_pe4.h, s_pe4.v)
                A.copy(h_all[:, tc * 32:(tc + 1) * 32], ps_h[:, 0:32])
                A.drain()
                s_a4.inc(A.copy(tmpf_all[:, tc * 128:(tc + 1) * 128],
                                ps_tmp[:, 0:128]))
                a4_gate_prev = s_a4.v

            SC.wait_ge(s_v4.h, stt_v4[1])
            s_x2s[1].inc(SC.dma_start(
                out=x2_d[(TC - 1) * 128:TC * 128, :], in_=x2_sb[1][:, :]), 16)
            loopend = {s: s.v for s in (s_v4, s_a4, s_pe4,
                                        s_l2[0], s_l2[1], s_x2s[0], s_x2s[1])}

            hv = h_all[:, :].rearrange("p (c g) -> p c g", c=TC)
            hqv = hsq[:, :].rearrange("p (c g) -> p c g", c=TC)
            gb1v = gb1B[:, :].unsqueeze(1).broadcast_to([128, TC, 32])
            gamv = gamB[:, :].unsqueeze(1).broadcast_to([128, TC, 32])
            betv = betB[:, :].unsqueeze(1).broadcast_to([128, TC, 32])
            V.wait_ge(s_a4.h, s_a4.v)
            V.tensor_tensor(hv, hv, gb1v, OP.add)
            V.drain()
            V.tensor_reduce(hmu[:, :], hv, axis=AX, op=OP.add)
            V.drain()
            V.tensor_scalar(hmu[:, :], hmu[:, :], -1.0 / 32.0, None, OP.mult)
            V.drain()
            V.tensor_tensor(hv, hv,
                            hmu[:, :].unsqueeze(2).broadcast_to([128, TC, 32]),
                            OP.add)
            V.drain()
            V.tensor_tensor(hqv, hv, hv, OP.mult)
            V.drain()
            V.tensor_reduce(hvar[:, :], hqv, axis=AX, op=OP.add)
            V.drain()
            s_v4.inc(V.tensor_scalar(hvar[:, :], hvar[:, :], 1.0 / 32.0,
                                     None, OP.mult))
            A.wait_ge(s_v4.h, s_v4.v)
            A.activation(hrs[:, :], hvar[:, :], AF.Ln, bias=eps_t[:, 0:1], scale=1.0)
            A.drain()
            s_a4.inc(A.activation(hrs[:, :], hrs[:, :], AF.Exp, scale=-0.5))
            V.wait_ge(s_a4.h, s_a4.v)
            V.tensor_tensor(hv, hv,
                            hrs[:, :].unsqueeze(2).broadcast_to([128, TC, 32]),
                            OP.mult)
            V.drain()
            V.tensor_tensor(hv, hv, gamv, OP.mult)
            V.drain()
            V.tensor_tensor(hv, hv, betv, OP.add)
            V.drain()
            V.tensor_scalar(h_all[:, :], h_all[:, :], 0.0, None, OP.max)
            V.drain()
            s_v4.inc(V.tensor_copy(hb_all[:, :], h_all[:, :]))
            PE.wait_ge(s_v4.h, s_v4.v)
            for tc in range(TC):
                s_pe4.inc(PE.transpose(ps_hT4[:, tc * 128:(tc + 1) * 128],
                                       hb_all[:, tc * 32:(tc + 1) * 32],
                                       ident[:, :]))
            A.wait_ge(s_pe4.h, s_pe4.v)
            s_a4.inc(A.copy(hT_all[:, :], ps_hT4[:, :]))
            PE.wait_ge(s_a4.h, s_a4.v)
            mmi = None
            for tc in range(TC):
                mmi = PE.matmul(ps_g[:, tc * 8:(tc + 1) * 8],
                                hT_all[0:32, tc * 128:(tc + 1) * 128],
                                gw2_sb[0:32, :],
                                start=True, stop=True)
            s_pe4.inc(mmi)
            A.wait_ge(s_pe4.h, s_pe4.v)
            s_a4.inc(A.copy(gat_all[:, :], ps_g[:, 0:128]))
            gv = gat_all[:, :].rearrange("p (c e) -> p c e", c=TC)
            gv2 = gat2_all[:, :].rearrange("p (c e) -> p c e", c=TC)
            e1v = eq1a[:, :].rearrange("p (c e) -> p c e", c=TC)
            e2v = eq2a[:, :].rearrange("p (c e) -> p c e", c=TC)
            V.wait_ge(s_a4.h, s_a4.v)
            V.tensor_tensor(gv, gv,
                            gb2B[:, :].unsqueeze(1).broadcast_to([128, TC, 8]),
                            OP.add)
            V.drain()
            V.reduce_max(m1a[:, :], gv, axis=AX)
            V.drain()
            V.tensor_tensor(e1v, gv,
                            m1a[:, :].unsqueeze(2).broadcast_to([128, TC, 8]),
                            OP.is_equal)
            V.drain()
            V.scalar_tensor_tensor(gv2, e1v, BIG_NEG, gv, OP.mult, OP.add)
            V.drain()
            V.reduce_max(m2a[:, :], gv2, axis=AX)
            V.drain()
            V.tensor_tensor(e2v, gv2,
                            m2a[:, :].unsqueeze(2).broadcast_to([128, TC, 8]),
                            OP.is_equal)
            V.drain()
            ebv = eqb_all[:, :].rearrange("p (c k) -> p c k", c=TC)
            V.tensor_copy(ebv[:, :, 0:8], e1v)
            V.drain()
            s_v4.inc(V.tensor_copy(ebv[:, :, 8:16], e2v))
            PE.wait_ge(s_v4.h, s_v4.v)
            s_pe4.inc(PE.matmul(ps_cnt2[0:1, 0:256], ones_b[:, :], eqb_all[:, :],
                                start=True, stop=True))
            A.wait_ge(s_pe4.h, s_pe4.v)
            s_a4.inc(A.copy(cntf[:, :], ps_cnt2[0:1, 0:256]))
            V.wait_ge(s_a4.h, s_a4.v)
            s_v4.inc(V.tensor_reduce(
                cnt_row[:, :], cntf[:, :].rearrange("p (c k) -> p k c", c=TC),
                axis=AX, op=OP.add))
            cnt_v4 = s_v4.v
            G.wait_ge(s_v4.h, cnt_v4)
            s_cci.inc(G.dma_start(out=cc_in.ap(), in_=cnt_row[:, :]), 16)
            G.wait_ge(s_cci.h, s_cci.v)
            G.collective_compute(
                "AllReduce", OP.add, replica_groups=[list(range(NCORES))],
                ins=[cc_in.ap().opt()], outs=[cc_out.ap().opt()],
            ).then_inc(s_ccd)
            s_v4.inc(V.scalar_tensor_tensor(dga[:, :], m1a[:, :], -1.0, m2a[:, :],
                                            OP.mult, OP.add))
            A.wait_ge(s_v4.h, s_v4.v)
            s_a4.inc(A.activation(sxa[:, :], dga[:, :], AF.Exp))
            V.wait_ge(s_a4.h, s_a4.v)
            V.tensor_scalar(spa[:, :], sxa[:, :], 1.0, None, OP.add)
            V.drain()
            V.reciprocal(rsa[:, :], spa[:, :])
            V.drain()
            V.tensor_tensor(p2a[:, :], sxa[:, :], rsa[:, :], OP.mult)
            V.drain()
            V.tensor_scalar(p1a[:, :], p2a[:, :], -ALPHA, ALPHA, OP.mult, OP.add)
            V.drain()
            V.tensor_scalar(p2a[:, :], p2a[:, :], ALPHA, None, OP.mult)
            V.drain()
            V.tensor_tensor(g1a[:, :].rearrange("p (c e) -> p c e", c=TC), e1v,
                            p1a[:, :].unsqueeze(2).broadcast_to([128, TC, 8]),
                            OP.mult)
            V.drain()
            V.tensor_tensor(g2a[:, :].rearrange("p (c e) -> p c e", c=TC), e2v,
                            p2a[:, :].unsqueeze(2).broadcast_to([128, TC, 8]),
                            OP.mult)
            V.drain()
            tv = tmpf_all[:, :].rearrange("p (c ee r) -> p c ee r", c=TC, ee=8)
            V.tensor_tensor(
                t1b_all[:, :].rearrange("p (c ee r) -> p c ee r", c=TC, ee=8), tv,
                g1a[:, :].rearrange("p (c e) -> p c e", c=TC)
                .unsqueeze(3).broadcast_to([128, TC, 8, 16]), OP.mult)
            V.drain()
            s_v4.inc(V.tensor_tensor(
                t2b_all[:, :].rearrange("p (c ee r) -> p c ee r", c=TC, ee=8), tv,
                g2a[:, :].rearrange("p (c e) -> p c e", c=TC)
                .unsqueeze(3).broadcast_to([128, TC, 8, 16]), OP.mult))
            tp_base = s_tp4.v
            PE.wait_ge(s_v4.h, s_v4.v)
            PE.wait_ge(s_tp4.h, tp_base)
            tpi = 0
            for tsrc, tdst in ((t1b_all, t1_sb), (t2b_all, t2_sb)):
                for tc in range(TC):
                    bank = (tpi // 8) % 2
                    slot = tpi % 8
                    if slot == 0 and tpi >= 16:
                        PE.wait_ge(s_tp4.h, tp_base + tpi // 8 - 1)
                    s_pe4.inc(PE.transpose(
                        ps_tp4[bank][:, slot * 128:(slot + 1) * 128],
                        tsrc[:, tc * 128:(tc + 1) * 128], ident[:, :]))
                    if slot == 7:
                        A.wait_ge(s_pe4.h, s_pe4.v)
                        s_tp4.inc(A.copy(tdst[:, (tc - 7) * 128:(tc + 1) * 128],
                                         ps_tp4[bank][:, 0:1024]))
                    tpi += 1

        es_bflat = ExitStack()
        ebf = es_bflat.enter_context
        tot = ebf(nc.sbuf_tensor("tot", [1, 16], F32))
        alw = ebf(nc.sbuf_tensor("alw", [1, 16], F32))
        alw_rep = ebf(nc.sbuf_tensor("alw_rep", [1, 256], F32))
        a0x = ebf(nc.sbuf_tensor("a0x", [128, 1], F32))
        a1x = ebf(nc.sbuf_tensor("a1x", [128, 1], F32))
        estats = ebf(nc.sbuf_tensor("estats", [128, TC * NTN * 6], F32))
        tall = ebf(nc.sbuf_tensor("tall", [128, TOK], BF16))

        es_q = ExitStack()
        b_sb = es_q.enter_context(nc.sbuf_tensor("b_sb", [128, D], BF16))
        l2T = es_q.enter_context(nc.sbuf_tensor("l2T", [128, KC * TOK], BF16))

        l2Tv2 = l2T[:, :].rearrange("p (kc t) -> p kc t", kc=KC)
        for Q in (SY, SC):
            Q.wait_ge(s_v4.h, s_v4.v)
            Q.wait_ge(s_a4.h, s_a4.v)
            Q.wait_ge(s_pe4.h, s_pe4.v)
            Q.wait_ge(s_l2[0].h, s_l2[0].v)
            Q.wait_ge(s_l2[1].h, s_l2[1].v)
            Q.wait_ge(s_x2s[0].h, s_x2s[0].v)
            Q.wait_ge(s_x2s[1].h, s_x2s[1].v)

        with ExitStack() as es3:
            wf = [es3.enter_context(nc.sbuf_tensor(f"wf{i}", [128, KH * NTW], BF16))
                  for i in range(NWS)]
            ps_acc = [es3.enter_context(nc.psum_tensor(f"ps_accB{i}", [128, 512], F32))
                      for i in range(8)]

            def wf_load(sidx):
                kh = sidx % 2
                nt = sidx // 2
                wsl = sidx % NWS
                if sidx >= NWS:
                    G.wait_ge(s_wffB[wsl].h, s_wffB[wsl].v)
                s_wfB[wsl].inc(G.dma_start(
                    out=wf[wsl][:, :].rearrange("p (kc n) -> p kc n", kc=KH),
                    in_=wfT_d[kh * KH * 128:(kh + 1) * KH * 128,
                              nt * NTW:(nt + 1) * NTW]
                        .rearrange("(kc kp) n -> kp kc n", kp=128)), 16)

            G.wait_ge(s_mm.h, s_mm.v)
            G.wait_ge(s_aev.h, s_aev.v)
            for i in range(8):
                G.wait_ge(s_stg[i].h, s_stg[i].v)
            for s, v in loopend.items():
                G.wait_ge(s.h, v)
            wf_load(0)
            wf_load(1)
            s_q1.inc(G.dma_start(out=b_sb[:, :], in_=bRe_d), 16)
            wf_load(2)
            G.wait_ge(s_ccd, 1)
            s_cco.inc(G.dma_start(out=tot[:, :], in_=cc_out.ap()), 16)

            l2mark = [None] * TC
            for tc in range(TC):
                Q, sem = (SY, s_l2B) if tc % 2 == 0 else (SC, s_l2C)
                sem.inc(Q.dma_start(
                    out=l2Tv2[:, :, tc * 128:(tc + 1) * 128],
                    in_=l2t_d[tc * 128:(tc + 1) * 128, :]
                        .rearrange("p (kc t) -> p kc t", kc=KC)), 16)
                l2mark[tc] = (sem, sem.v)

            V.wait_ge(s_cco.h, s_cco.v)
            V.tensor_scalar(alw[:, :], tot[:, :], CAPACITY + 0.5, None, OP.is_lt)
            V.drain()
            s_v3B.inc(V.tensor_copy(
                alw_rep[:, :].rearrange("p (k e r) -> p k e r", k=2, e=8),
                alw[:, :].rearrange("p (k e) -> p k e", k=2)
                .unsqueeze(3).broadcast_to([1, 2, 8, 16])))
            G.wait_ge(s_v3B.h, s_v3B.v)
            s_awx.inc(G.dma_start(out=awd_dram.ap(), in_=alw_rep[:, :]), 16)
            G.wait_ge(s_awx.h, 16)
            awd_col = awd_dram.ap().rearrange("o (a b) -> (o a) b", b=1)
            s_awx.inc(G.dma_start(out=a0x[:, :], in_=awd_col[0:128, :]), 16)
            s_awx.inc(G.dma_start(out=a1x[:, :], in_=awd_col[128:256, :]), 16)
            V.wait_ge(s_awx.h, s_awx.v)
            V.wait_ge(s_tp4.h, s_tp4.v)
            V.tensor_scalar(tall[:, :], t2_sb[:, :], a1x[:, 0:1], None, OP.mult)
            V.drain()
            s_v3B.inc(V.scalar_tensor_tensor(tall[:, :], t1_sb[:, :], a0x[:, 0:1],
                                             tall[:, :], OP.mult, OP.add))
            es_t14.close()
            stage = [es3.enter_context(nc.sbuf_tensor(f"stgB{i}", [128, NTW], BF16))
                     for i in range(8)]

            grp = 0
            for nt in range(NTN):
                for kh in range(2):
                    sidx = nt * 2 + kh
                    if sidx >= 3:
                        wf_load(sidx)
                for tg in range(TG):
                    bank = (grp % 2) * 4
                    if nt == 0:
                        for tci in range(4):
                            sem, val = l2mark[tg * 4 + tci]
                            PE.wait_ge(sem.h, val)
                    if grp == 0:
                        PE.wait_ge(s_q1.h, s_q1.v)
                    if grp >= 2:
                        PE.wait_ge(s_aevB.h, 4 * (grp - 1))
                    for kc in range(KC):
                        wsl = (nt * 2 + kc // KH) % NWS
                        if kc % KH == 0:
                            PE.wait_ge(s_wfB[wsl].h, s_wfB[wsl].v)
                        for tci in range(4):
                            tc = tg * 4 + tci
                            PE.matmul(
                                ps_acc[bank + tci][:, 0:NTW],
                                l2T[:, kc * TOK + tc * 128:kc * TOK + (tc + 1) * 128],
                                wf[wsl][:, (kc % KH) * NTW:(kc % KH + 1) * NTW],
                                start=(kc == 0), stop=False)
                    if grp == 0:
                        PE.wait_ge(s_v3B.h, s_v3B.v)
                    mmi = None
                    for tci in range(4):
                        tc = tg * 4 + tci
                        mmi = PE.matmul(ps_acc[bank + tci][:, 0:NTW],
                                        tall[:, tc * 128:(tc + 1) * 128],
                                        b_sb[:, nt * NTW:(nt + 1) * NTW],
                                        start=False, stop=True)
                    s_mmB.inc(mmi)
                    if tg == TG - 1:
                        s_wffB[(nt * 2) % NWS].inc(PE.nop())
                        s_wffB[(nt * 2 + 1) % NWS].inc(PE.nop())
                    A.wait_ge(s_mmB.h, s_mmB.v)
                    for tci in range(4):
                        tc = tg * 4 + tci
                        ssl = (grp % 2) * 4 + tci
                        uses = grp // 2
                        if uses >= 1:
                            A.wait_ge(s_stgB[ssl].h, 17 * uses)
                        s_aevB.inc(A.copy(stage[ssl][:, :], ps_acc[bank + tci][:, 0:NTW]))
                        SC.wait_ge(s_aevB.h, s_aevB.v)
                        s_stgB[ssl].inc(SC.dma_start(
                            out=e_dram.ap()[tc * 128:(tc + 1) * 128,
                                            nt * NTW:(nt + 1) * NTW],
                            in_=stage[ssl][:, :]), 16)
                        V.wait_ge(s_aevB.h, s_aevB.v)
                        s_stgB[ssl].inc(V.bn_stats(
                            estats[:, (tc * NTN + nt) * 6:(tc * NTN + nt) * 6 + 6],
                            stage[ssl][:, :]))
                    grp += 1
        es_q.close()

        with ExitStack() as es4B:
            ebuf = [es4B.enter_context(nc.sbuf_tensor(f"eb{i}", [128, D], BF16))
                    for i in range(6)]
            x2b = [es4B.enter_context(nc.sbuf_tensor(f"x2b{i}", [128, D], F32))
                   for i in range(4)]
            o_sb = [es4B.enter_context(nc.sbuf_tensor(f"o_sb{i}", [128, D], F32))
                    for i in range(4)]
            mvbB = es4B.enter_context(nc.sbuf_tensor("mvbB", [128, 2 * TC], F32))
            rsbB = es4B.enter_context(nc.sbuf_tensor("rsbB", [128, TC], F32))
            nmbB = es4B.enter_context(nc.sbuf_tensor("nmbB", [128, TC], F32))

            for Q in (SY, SC, G):
                Q.wait_ge(s_aevB.h, s_aevB.v)
                for i in range(8):
                    Q.wait_ge(s_stgB[i].h, s_stgB[i].v)
            for tc in range(TC):
                V.bn_aggr(mvbB[:, 2 * tc:2 * tc + 2], estats[:, tc * 48:(tc + 1) * 48])
                V.drain()
            mview = mvbB[:, :].rearrange("p (t two) -> p t two", two=2)
            s_v3B.inc(V.tensor_scalar(nmbB[:, :], mview[:, :, 0:1].squeeze(2),
                                      -1.0, None, OP.mult))
            A.wait_ge(s_v3B.h, s_v3B.v)
            A.activation(rsbB[:, :], mview[:, :, 1:2].squeeze(2),
                         AF.Ln, bias=eps_t[:, 0:1], scale=1.0)
            A.drain()
            s_a3B.inc(A.activation(rsbB[:, :], rsbB[:, :], AF.Exp, scale=-0.5))
            V.wait_ge(s_a3B.h, s_a3B.v)
            s_v3B.inc(V.tensor_tensor(nmbB[:, :], nmbB[:, :], rsbB[:, :], OP.mult))
            V.drain()
            s_e = s_peB[0]
            s_xq = {SC: s_pxxB[0], G: s_pxxB[1], SY: s_p3B}
            s_oq = {G: s_outB, SC: s_peB[1]}
            cmp_chunk = [0] * TC
            st_mark = [None] * TC
            emark = [0] * TC
            xmark = [None] * TC

            def emit_store(j):
                OQ = G if j % 2 == 0 else SC
                so = s_oq[OQ]
                OQ.wait_ge(s_v3B.h, cmp_chunk[j])
                so.inc(OQ.dma_start(
                    out=out_d[j * 128:(j + 1) * 128, :],
                    in_=o_sb[j % 4][:, :]), 16)
                st_mark[j] = (so, so.v)

            for tc in range(TC):
                sl = tc % 4
                esl = tc % 6
                XQ = (SC, G, SY)[tc % 3]
                if tc >= 1:
                    emit_store(tc - 1)
                if tc >= 6:
                    SY.wait_ge(s_v3B.h, cmp_chunk[tc - 6])
                s_e.inc(SY.dma_start(
                    out=ebuf[esl][:, :],
                    in_=e_dram.ap()[tc * 128:(tc + 1) * 128, :]), 16)
                emark[tc] = s_e.v
                if tc >= 4:
                    XQ.wait_ge(s_v3B.h, cmp_chunk[tc - 4])
                sx = s_xq[XQ]
                sx.inc(XQ.dma_start(
                    out=x2b[sl][:, :], in_=x2_d[tc * 128:(tc + 1) * 128, :]), 16)
                xmark[tc] = (sx, sx.v)
                V.wait_ge(xmark[tc][0].h, xmark[tc][1])
                V.tensor_scalar(x2b[sl][:, :], x2b[sl][:, :],
                                nmbB[:, tc:tc + 1], None, OP.add)
                V.drain()
                V.wait_ge(s_e.h, emark[tc])
                if tc >= 4:
                    V.wait_ge(st_mark[tc - 4][0].h, st_mark[tc - 4][1])
                s_v3B.inc(V.scalar_tensor_tensor(
                    o_sb[sl][:, :], ebuf[esl][:, :], rsbB[:, tc:tc + 1],
                    x2b[sl][:, :], OP.mult, OP.add))
                cmp_chunk[tc] = s_v3B.v
            emit_store(TC - 1)
            SC.wait_ge(s_oq[G].h, s_oq[G].v)
            SC.wait_ge(s_oq[SC].h, s_oq[SC].v)
        es_bflat.close()

    return nc



def kernel(L0_attn_data, L0_attn_weight, L0_attn_mask, L0_ffn_weight,
           L0_ffn_lora_A, L0_ffn_lora_B, L0_ffn_gw1, L0_ffn_gb1,
           L0_ffn_ln_gamma, L0_ffn_ln_beta, L0_ffn_gw2, L0_ffn_gb2):
    x = np.asarray(L0_attn_data, dtype=np.float32)
    waT = np.ascontiguousarray(np.asarray(L0_attn_weight, np.float32).T)
    wfT = np.ascontiguousarray(np.asarray(L0_ffn_weight, np.float32).T)
    mask = np.asarray(L0_attn_mask, np.int32)
    maskR = np.ascontiguousarray(
        mask.reshape(128, 32, 32, 4, 32).transpose(3, 2, 0, 4, 1).reshape(128, 128, 1024))
    aRe = np.ascontiguousarray(
        np.asarray(L0_ffn_lora_A, np.float32).transpose(1, 0, 2).reshape(D, 128))
    bRe = np.ascontiguousarray(np.asarray(L0_ffn_lora_B, np.float32).reshape(128, D))
    gw1T = np.ascontiguousarray(np.asarray(L0_ffn_gw1, np.float32).T)
    gw2T = np.ascontiguousarray(np.tile(np.asarray(L0_ffn_gw2, np.float32).T, (4, 1)))
    gb1B = np.ascontiguousarray(np.broadcast_to(
        np.asarray(L0_ffn_gb1, np.float32), (128, 32)))
    gamB = np.ascontiguousarray(np.broadcast_to(
        np.asarray(L0_ffn_ln_gamma, np.float32), (128, 32)))
    betB = np.ascontiguousarray(np.broadcast_to(
        np.asarray(L0_ffn_ln_beta, np.float32), (128, 32)))
    gb2B = np.ascontiguousarray(np.broadcast_to(
        np.asarray(L0_ffn_gb2, np.float32), (128, 8)))

    nc = build()
    in_maps = [{
        "x": np.ascontiguousarray(x[c]),
        "waT": waT, "maskR": np.ascontiguousarray(maskR[:, 16 * c:16 * (c + 1), :]),
        "aRe": aRe,
        "gw1T": gw1T, "gw2T": gw2T,
        "gb1B": gb1B, "gamB": gamB, "betB": betB, "gb2B": gb2B,
        "wfT": wfT, "bRe": bRe,
    } for c in range(NCORES)]
    res = run_bass_kernel_spmd(nc, in_maps, core_ids=list(range(NCORES)))

    out = np.stack([res.results[c]["out"] for c in range(NCORES)], axis=0)
    return np.ascontiguousarray(out, dtype=np.float32)


# version 22 (final)
# speedup vs baseline: 1.0341x; 1.0186x over previous
import numpy as np
from contextlib import ExitStack

import concourse.bass as bass
import concourse.mybir as mybir
from concourse.bass_utils import run_bass_kernel_spmd
from concourse.alu_op_type import AluOpType as OP

F32 = mybir.dt.float32
BF16 = mybir.dt.bfloat16
I32 = mybir.dt.int32
AF = mybir.ActivationFunctionType
AX = mybir.AxisListType.X

NCORES = 8
TOK = 2048
TC = 16
D = 4096
KC = 32
KH = 16
NTN = 8
NTW = 512
TG = 4
NWS = 3
EPS = 1e-5
CAPACITY = float(int(3.0 * (NCORES * TOK) / 8))
BIG_NEG = -1.0e30
ALPHA = 1.0 / 16.0


class Sem:

    def __init__(self, es, nc, name):
        self.h = es.enter_context(nc.semaphore(name))
        self.v = 0

    def inc(self, instr, k=1):
        instr.then_inc(self.h, k)
        self.v += k
        return instr


def ln_chain(V, A, x_ap, out_ap, st, mv, nm, nmr, rstd, eps_t, sv, sa, nstats,
             APL=None):
    APL = APL if APL is not None else V
    w = D // nstats
    for i in range(nstats):
        V.bn_stats(st[:, 6 * i:6 * i + 6], x_ap[:, w * i:w * (i + 1)])
    V.drain()
    V.bn_aggr(mv[:, :], st[:, 0:6 * nstats])
    V.drain()
    sv.inc(V.tensor_scalar(nm[:, :], mv[:, 0:1], -1.0, None, OP.mult))
    A.wait_ge(sv.h, sv.v)
    A.activation(rstd[:, :], mv[:, 1:2], AF.Ln, bias=eps_t[:, 0:1], scale=1.0)
    A.drain()
    sa.inc(A.activation(rstd[:, :], rstd[:, :], AF.Exp, scale=-0.5))
    APL.wait_ge(sa.h, sa.v)
    if APL is not V:
        APL.wait_ge(sv.h, sv.v)
    APL.tensor_tensor(nmr[:, :], nm[:, :], rstd[:, :], OP.mult)
    if APL is V:
        APL.drain()
    return APL.tensor_scalar(out_ap, x_ap, rstd[:, 0:1], nmr[:, 0:1], OP.mult, OP.add)



def build():
    nc = bass.Bass("TRN2", target_bir_lowering=False, debug=False)

    x_d = nc.dram_tensor("x", [TOK, D], F32, kind="ExternalInput").ap()
    waT_d = nc.dram_tensor("waT", [D, D], F32, kind="ExternalInput").ap()
    maskR_d = nc.dram_tensor("maskR", [128, 16, 1024], I32, kind="ExternalInput").ap()
    aRe_d = nc.dram_tensor("aRe", [D, 128], F32, kind="ExternalInput").ap()
    gw1T_d = nc.dram_tensor("gw1T", [D, 32], F32, kind="ExternalInput").ap()
    gw2T_d = nc.dram_tensor("gw2T", [128, 8], F32, kind="ExternalInput").ap()
    gb1B_d = nc.dram_tensor("gb1B", [128, 32], F32, kind="ExternalInput").ap()
    gamB_d = nc.dram_tensor("gamB", [128, 32], F32, kind="ExternalInput").ap()
    betB_d = nc.dram_tensor("betB", [128, 32], F32, kind="ExternalInput").ap()
    gb2B_d = nc.dram_tensor("gb2B", [128, 8], F32, kind="ExternalInput").ap()
    wfT_d = nc.dram_tensor("wfT", [D, D], F32, kind="ExternalInput").ap()
    bRe_d = nc.dram_tensor("bRe", [128, D], F32, kind="ExternalInput").ap()

    out_d = nc.dram_tensor("out", [TOK, D], F32, kind="ExternalOutput").ap()

    b_dram = nc.dram_tensor("b_scr", [TOK, D], BF16)
    bin_dram = nc.dram_tensor("bin_scr", [128, 128], BF16)
    x2_d = nc.dram_tensor("x2_scr", [TOK, D], F32).ap()
    l2t_d = nc.dram_tensor("l2t_scr", [TOK, D], BF16).ap()
    e_dram = nc.dram_tensor("e_scr", [TOK, D], BF16)
    cc_in = nc.dram_tensor("cc_in", [1, 16], F32)
    cc_out = nc.dram_tensor("cc_out", [1, 16], F32)
    awd_dram = nc.dram_tensor("awd_scr", [1, 256], F32)
    ag_in = nc.dram_tensor("ag_in", [128, 16], F32)
    ag_out = nc.dram_tensor("ag_out", [1024, 16], F32)

    PE, V, A, G, SY, SC = nc.tensor, nc.vector, nc.scalar, nc.gpsimd, nc.sync, nc.scalar

    with ExitStack() as es:
        e = es.enter_context

        ident = e(nc.sbuf_tensor("ident", [128, 128], BF16))
        e4t = e(nc.sbuf_tensor("e4t", [4, 128], BF16))
        eps_t = e(nc.sbuf_tensor("eps_t", [128, 1], F32))
        ones_b = e(nc.sbuf_tensor("ones_b", [128, 1], BF16))
        gw2_sb = e(nc.sbuf_tensor("gw2_sb", [128, 8], BF16))
        gb1B = e(nc.sbuf_tensor("gb1B_s", [128, 32], F32))
        gamB = e(nc.sbuf_tensor("gamB_s", [128, 32], F32))
        betB = e(nc.sbuf_tensor("betB_s", [128, 32], F32))
        gb2B = e(nc.sbuf_tensor("gb2B_s", [128, 8], F32))
        stats = [e(nc.sbuf_tensor(f"st{i}", [128, 96], F32)) for i in range(2)]
        mv = [e(nc.sbuf_tensor(f"mv{i}", [128, 2], F32)) for i in range(2)]
        nm = [e(nc.sbuf_tensor(f"nm{i}", [128, 1], F32)) for i in range(2)]
        nmr = [e(nc.sbuf_tensor(f"nmr{i}", [128, 1], F32)) for i in range(2)]
        rstd = [e(nc.sbuf_tensor(f"rs{i}", [128, 1], F32)) for i in range(2)]

        s_const = Sem(es, nc, "s_const")
        s_cpool = Sem(es, nc, "s_cpool")
        s_ld1 = Sem(es, nc, "s_ld1")
        s_msk = [Sem(es, nc, f"s_msk{i}") for i in range(2)]
        s_mskr = Sem(es, nc, "s_mskr")
        s_bin = Sem(es, nc, "s_bin")
        s_binp = Sem(es, nc, "s_binp")
        s_binpm = Sem(es, nc, "s_binpm")
        s_ldx = [Sem(es, nc, f"s_ldx{i}") for i in range(2)]
        s_ldxg = Sem(es, nc, "s_ldxg")
        s_abf = [Sem(es, nc, f"s_abf{i}") for i in range(2)]
        s_aused = Sem(es, nc, "s_aused")
        s_lnv = Sem(es, nc, "s_lnv")
        s_lna = Sem(es, nc, "s_lna")
        s_pt = Sem(es, nc, "s_pt")
        s_ev = Sem(es, nc, "s_ev")
        s_wm = [Sem(es, nc, f"s_wm{i}") for i in range(NWS)]
        s_wmf = [Sem(es, nc, f"s_wmf{i}") for i in range(NWS)]
        s_wa = [Sem(es, nc, f"s_wa{i}") for i in range(NWS)]
        s_mm = Sem(es, nc, "s_mm")
        s_aev = Sem(es, nc, "s_aev")
        s_stg = [Sem(es, nc, f"s_stg{i}") for i in range(8)]
        s_pb = [Sem(es, nc, f"s_pb{i}") for i in range(2)]
        s_px = [Sem(es, nc, f"s_px{i}") for i in range(2)]
        s_v4 = Sem(es, nc, "s_v4")
        s_a4 = Sem(es, nc, "s_a4")
        s_p4 = Sem(es, nc, "s_p4")
        s_pe4 = Sem(es, nc, "s_pe4")
        s_tp4 = Sem(es, nc, "s_tp4")
        s_l2u = Sem(es, nc, "s_l2u")
        s_x2s = [Sem(es, nc, f"s_x2s{i}") for i in range(2)]
        s_l2 = [Sem(es, nc, f"s_l2{i}") for i in range(2)]
        s_cci = Sem(es, nc, "s_cci")
        s_cco = Sem(es, nc, "s_cco")
        s_ccd = e(nc.semaphore("s_ccd"))
        s_bsi = Sem(es, nc, "s_bsi")
        s_bsd = Sem(es, nc, "s_bsd")
        s_ccd2 = e(nc.semaphore("s_ccd2"))
        s_q1 = Sem(es, nc, "s_q1")
        s_awx = Sem(es, nc, "s_awx")
        s_l2B = Sem(es, nc, "s_l2B")
        s_l2C = Sem(es, nc, "s_l2C")
        s_wfB = [Sem(es, nc, f"s_wfB{i}") for i in range(NWS)]
        s_wffB = [Sem(es, nc, f"s_wffB{i}") for i in range(NWS)]
        s_mmB = Sem(es, nc, "s_mmB")
        s_aevB = Sem(es, nc, "s_aevB")
        s_stgB = [Sem(es, nc, f"s_stgB{i}") for i in range(8)]
        s_peB = [Sem(es, nc, f"s_peB{i}") for i in range(2)]
        s_pxxB = [Sem(es, nc, f"s_pxxB{i}") for i in range(2)]
        s_v3B = Sem(es, nc, "s_v3B")
        s_a3B = Sem(es, nc, "s_a3B")
        s_p3B = Sem(es, nc, "s_p3B")
        s_outB = Sem(es, nc, "s_outB")

        blk = e(nc.Block())

        G.memset(ident[:, :], 0.0)
        G.drain()
        G.affine_select(out=ident[:, :], in_=ident[:, :], compare_op=OP.not_equal,
                        fill=1.0, base=0, pattern=[[-1, 128]], channel_multiplier=1)
        G.drain()
        G.memset(e4t[:, :], 1.0)
        G.drain()
        G.affine_select(out=e4t[:, :], in_=e4t[:, :], compare_op=OP.is_ge,
                        fill=0.0, base=0, pattern=[[1, 128]], channel_multiplier=-32)
        G.drain()
        G.affine_select(out=e4t[:, :], in_=e4t[:, :], compare_op=OP.is_ge,
                        fill=0.0, base=31, pattern=[[-1, 128]], channel_multiplier=32)
        G.drain()
        s_cpool.inc(G.memset(ones_b[:, :], 1.0))
        s_const.inc(G.dma_start(out=gw2_sb[:, :], in_=gw2T_d), 16)
        s_ld1.inc(SY.dma_start(out=gb1B[:, :], in_=gb1B_d), 16)
        s_ld1.inc(SY.dma_start(out=gamB[:, :], in_=gamB_d), 16)
        s_ld1.inc(SY.dma_start(out=betB[:, :], in_=betB_d), 16)
        s_ld1.inc(SY.dma_start(out=gb2B[:, :], in_=gb2B_d), 16)
        V.memset(eps_t[:, :], EPS)
        V.drain()

        es_big = ExitStack()
        aT = es_big.enter_context(nc.sbuf_tensor("aT", [128, KC * TOK], BF16))
        binP = es_big.enter_context(nc.sbuf_tensor("binP", [128, 4096], BF16))
        bs_part = es_big.enter_context(nc.sbuf_tensor("bs_part", [128, 16], F32))
        bs_all = es_big.enter_context(nc.sbuf_tensor("bs_all", [128, 128], F32))
        binX = es_big.enter_context(nc.sbuf_tensor("binX", [128, 128], BF16))
        es_bst = ExitStack()

        es_msk = ExitStack()
        msk = [es_msk.enter_context(nc.sbuf_tensor(f"msk{i}", [128, 4096], BF16))
               for i in range(2)]

        with ExitStack() as es2:
            xbuf = [es2.enter_context(nc.sbuf_tensor(f"xb{i}", [128, D], F32))
                    for i in range(2)]
            abf = [es2.enter_context(nc.sbuf_tensor(f"ab{i}", [128, D], BF16))
                   for i in range(2)]
            ps_tp = [es2.enter_context(nc.psum_tensor(f"ps_tp{i}", [128, 1024], BF16))
                     for i in range(2)]

            aTv = aT[:, :].rearrange("p (kc t) -> p kc t", kc=KC)
            for tc in range(TC):
                sl = tc % 2
                XQ2 = SY if (tc < 8 or tc % 2 == 0) else G
                if tc >= 2:
                    XQ2.wait_ge(s_abf[sl].h, s_abf[sl].v)
                s_ldc = s_ldxg if XQ2 is G else s_ldx[sl]
                s_ldc.inc(XQ2.dma_start(
                    out=xbuf[sl][:, :], in_=x_d[tc * 128:(tc + 1) * 128, :]), 16)
                V.wait_ge(s_ldc.h, s_ldc.v)
                if tc >= 2:
                    V.wait_ge(s_aused.h, tc - 1)
                ap = ln_chain(V, A, xbuf[sl][:, :], abf[sl][:, :], stats[sl], mv[sl],
                              nm[sl], nmr[sl], rstd[sl], eps_t, s_lnv, s_lna, 8)
                s_abf[sl].inc(ap)
                if tc == 0:
                    PE.wait_ge(s_const.h, s_const.v)
                PE.wait_ge(s_abf[sl].h, s_abf[sl].v)
                for kc in range(KC):
                    j = tc * KC + kc
                    bank = (j // 8) % 2
                    slot = j % 8
                    if j >= 16 and slot == 0:
                        PE.wait_ge(s_ev.h, j // 8 - 1)
                    s_pt.inc(PE.transpose(ps_tp[bank][:, slot * 128:(slot + 1) * 128],
                                          abf[sl][:, kc * 128:(kc + 1) * 128],
                                          ident[:, :]))
                    if kc == KC - 1:
                        s_aused.inc(PE.nop())
                    if slot == 7:
                        A.wait_ge(s_pt.h, s_pt.v)
                        s_ev.inc(A.copy(aTv[:, kc - 7:kc + 1, tc * 128:(tc + 1) * 128],
                                        ps_tp[bank][:, :].rearrange(
                                            "p (s t) -> p s t", s=8)))
                if tc < 4:
                    msl = tc % 2
                    if tc >= 2:
                        G.wait_ge(s_mskr.h, tc - 1)
                    s_msk[msl].inc(G.dma_start(
                        out=msk[msl][:, :].rearrange("p (a j) -> p a j", a=4),
                        in_=maskR_d[:, 4 * tc:4 * (tc + 1), :]), 16)
                    V.wait_ge(s_msk[msl].h, s_msk[msl].v)
                    s_mskr.inc(V.tensor_reduce(
                        bs_part[:, 4 * tc:4 * (tc + 1)],
                        msk[msl][:, :].rearrange("p (a j) -> p a j", a=4),
                        axis=AX, op=OP.add))
                    V.drain()
                if tc == 3:
                    SC.wait_ge(s_mskr.h, s_mskr.v)
                    s_bsi.inc(SC.dma_start(out=ag_in.ap(), in_=bs_part[:, :]), 16)
                if tc == 4:
                    G.wait_ge(s_bsi.h, s_bsi.v)
                    G.collective_compute(
                        "AllGather", OP.bypass,
                        replica_groups=[list(range(NCORES))],
                        ins=[ag_in.ap().opt()], outs=[ag_out.ap().opt()],
                    ).then_inc(s_ccd2)
                if tc == 6:
                    SC.wait_ge(s_ccd2, 1)
                    s_bsd.inc(SC.dma_start(
                        out=bs_all[:, :].rearrange("p (r j) -> p r j", r=8),
                        in_=ag_out.ap().rearrange("(r p) j -> p r j", r=8)), 16)

        V.wait_ge(s_bsd.h, s_bsd.v)
        s_bin.inc(V.tensor_scalar(binX[:, :], bs_all[:, :], 0.0, None, OP.is_gt))
        SC.wait_ge(s_bin.h, s_bin.v)
        s_bin.inc(SC.dma_start(out=bin_dram.ap(), in_=binX[:, :]), 16)
        with ExitStack() as esb:
            bin_g = esb.enter_context(nc.sbuf_tensor("bin_g", [4, 4096], BF16))
            ps_bin = [esb.enter_context(nc.psum_tensor(f"ps_bin{i}", [128, 512], F32))
                      for i in range(2)]
            SY.wait_ge(s_bin.h, s_bin.v)
            s_bin.inc(SY.dma_start(
                out=bin_g[:, :].rearrange("g (kc nb) -> g kc nb", kc=KC),
                in_=bin_dram.ap().rearrange("(g kc) nb -> g kc nb", g=4)), 16)
            PE.wait_ge(s_bin.h, s_bin.v)
            bgv = bin_g[:, :].rearrange("g (kc nb) -> g kc nb", kc=KC)
            for nt in range(NTN):
                if nt >= 2:
                    PE.wait_ge(s_binp.h, nt - 1)
                s_binpm.inc(PE.matmul(ps_bin[nt % 2][:, 0:512], e4t[:, :],
                                      bgv[:, :, nt * 16:(nt + 1) * 16],
                                      start=True, stop=True))
                A.wait_ge(s_binpm.h, s_binpm.v)
                s_binp.inc(A.copy(binP[:, nt * 512:(nt + 1) * 512],
                                  ps_bin[nt % 2][:, 0:512]))
        es_msk.close()

        es_t14 = ExitStack()
        t1_sb = es_t14.enter_context(nc.sbuf_tensor("t1_sb", [128, TOK], BF16,
                                                    side="right"))
        t2_sb = es_t14.enter_context(nc.sbuf_tensor("t2_sb", [128, TOK], BF16,
                                                    side="right"))
        bstats = es_bst.enter_context(
            nc.sbuf_tensor("bstats", [128, TC * NTN * 6], F32, side="right"))
        with ExitStack() as es3:
            wm = [es3.enter_context(nc.sbuf_tensor(f"wm{i}", [128, KH * NTW], BF16))
                  for i in range(NWS)]
            stage = [es3.enter_context(nc.sbuf_tensor(f"stg{i}", [128, NTW], BF16))
                     for i in range(8)]
            ps_acc = [es3.enter_context(nc.psum_tensor(f"ps_acc{i}", [128, 512], F32))
                      for i in range(8)]

            grp = 0
            for nt in range(NTN):
                for kh in range(2):
                    sidx = nt * 2 + kh
                    wsl = sidx % NWS
                    if sidx == 0:
                        G.wait_ge(s_binp.h, s_binp.v)
                        G.wait_ge(s_mskr.h, s_mskr.v)
                    if sidx >= NWS:
                        G.wait_ge(s_wmf[wsl].h, s_wmf[wsl].v)
                    s_wa[wsl].inc(G.dma_start(
                        out=wm[wsl][:, :].rearrange("p (kc n) -> p kc n", kc=KH),
                        in_=waT_d[kh * KH * 128:(kh + 1) * KH * 128,
                                  nt * NTW:(nt + 1) * NTW]
                            .rearrange("(kc kp) n -> kp kc n", kp=128)), 16)
                    APL = G
                    APL.wait_ge(s_wa[wsl].h, s_wa[wsl].v)
                    APL.wait_ge(s_binp.h, s_binp.v)
                    bview = (binP[:, nt * 512 + kh * 256:nt * 512 + (kh + 1) * 256]
                             .rearrange("p (kc nb) -> p kc nb", kc=KH)
                             .unsqueeze(3).broadcast_to([128, KH, 16, 32]))
                    wview = wm[wsl][:, :].rearrange("p (kc nb n) -> p kc nb n",
                                                    kc=KH, nb=16)
                    s_wm[wsl].inc(APL.tensor_tensor(wview, wview, bview, OP.mult))

                for tg in range(TG):
                    bank = (grp % 2) * 4
                    if grp == 0:
                        PE.wait_ge(s_ev.h, s_ev.v)
                        PE.wait_ge(s_binp.h, s_binp.v)
                    if grp >= 2:
                        PE.wait_ge(s_aev.h, 4 * (grp - 1))
                    mmi = None
                    for kc in range(KC):
                        wsl = (nt * 2 + kc // KH) % NWS
                        if kc % KH == 0:
                            PE.wait_ge(s_wm[wsl].h, s_wm[wsl].v)
                        for tci in range(4):
                            tc = tg * 4 + tci
                            mmi = PE.matmul(
                                ps_acc[bank + tci][:, 0:NTW],
                                aT[:, kc * TOK + tc * 128:kc * TOK + (tc + 1) * 128],
                                wm[wsl][:, (kc % KH) * NTW:(kc % KH + 1) * NTW],
                                start=(kc == 0), stop=(kc == KC - 1))
                    s_mm.inc(mmi)
                    if tg == TG - 1:
                        s_wmf[(nt * 2) % NWS].inc(PE.nop())
                        s_wmf[(nt * 2 + 1) % NWS].inc(PE.nop())
                    A.wait_ge(s_mm.h, s_mm.v)
                    for tci in range(4):
                        tc = tg * 4 + tci
                        ssl = (grp % 2) * 4 + tci
                        uses = grp // 2
                        if uses >= 1:
                            A.wait_ge(s_stg[ssl].h, 17 * uses)
                        s_aev.inc(A.copy(stage[ssl][:, :], ps_acc[bank + tci][:, 0:NTW]))
                        SC.wait_ge(s_aev.h, s_aev.v)
                        s_stg[ssl].inc(SC.dma_start(
                            out=b_dram.ap()[tc * 128:(tc + 1) * 128,
                                            nt * NTW:(nt + 1) * NTW],
                            in_=stage[ssl][:, :]), 16)
                        V.wait_ge(s_aev.h, s_aev.v)
                        s_stg[ssl].inc(V.bn_stats(
                            bstats[:, (tc * NTN + nt) * 6:(tc * NTN + nt) * 6 + 6],
                            stage[ssl][:, :]))
                    grp += 1
        es_big.close()

        with ExitStack() as es4:
            gw1_sb = es4.enter_context(nc.sbuf_tensor("gw1_sb", [128, KC * 32], BF16))
            A_sb = es4.enter_context(nc.sbuf_tensor("A_sb", [128, KC * 128], BF16))
            mvb = es4.enter_context(nc.sbuf_tensor("mvb", [128, 2 * TC], F32))
            rsb = es4.enter_context(nc.sbuf_tensor("rsb", [128, TC], F32))
            nmb = es4.enter_context(nc.sbuf_tensor("nmb", [128, TC], F32))
            h_all = es4.enter_context(nc.sbuf_tensor("h_all", [128, TC * 32], F32))
            hsq = es4.enter_context(nc.sbuf_tensor("hsq", [128, TC * 32], F32))
            hmu = es4.enter_context(nc.sbuf_tensor("hmu", [128, TC], F32))
            hvar = es4.enter_context(nc.sbuf_tensor("hvar", [128, TC], F32))
            hrs = es4.enter_context(nc.sbuf_tensor("hrs", [128, TC], F32))
            hb_all = es4.enter_context(nc.sbuf_tensor("hb_all", [128, TC * 32], BF16))
            hT_all = es4.enter_context(nc.sbuf_tensor("hT_all", [32, TC * 128], BF16))
            gat_all = es4.enter_context(nc.sbuf_tensor("gat_all", [128, TC * 8], F32))
            gat2_all = es4.enter_context(nc.sbuf_tensor("gat2_all", [128, TC * 8], F32))
            m1a = es4.enter_context(nc.sbuf_tensor("m1a", [128, TC], F32))
            m2a = es4.enter_context(nc.sbuf_tensor("m2a", [128, TC], F32))
            dga = es4.enter_context(nc.sbuf_tensor("dga", [128, TC], F32))
            sxa = es4.enter_context(nc.sbuf_tensor("sxa", [128, TC], F32))
            spa = es4.enter_context(nc.sbuf_tensor("spa", [128, TC], F32))
            rsa = es4.enter_context(nc.sbuf_tensor("rsa", [128, TC], F32))
            p1a = es4.enter_context(nc.sbuf_tensor("p1a", [128, TC], F32))
            p2a = es4.enter_context(nc.sbuf_tensor("p2a", [128, TC], F32))
            eq1a = es4.enter_context(nc.sbuf_tensor("eq1a", [128, TC * 8], F32))
            eq2a = es4.enter_context(nc.sbuf_tensor("eq2a", [128, TC * 8], F32))
            g1a = es4.enter_context(nc.sbuf_tensor("g1a", [128, TC * 8], F32))
            g2a = es4.enter_context(nc.sbuf_tensor("g2a", [128, TC * 8], F32))
            eqb_all = es4.enter_context(nc.sbuf_tensor("eqb_all", [128, TC * 16], BF16))
            tmpf_all = es4.enter_context(nc.sbuf_tensor("tmpf_all", [128, TC * 128], F32))
            t1b_all = es4.enter_context(nc.sbuf_tensor("t1b_all", [128, TC * 128], BF16))
            t2b_all = es4.enter_context(nc.sbuf_tensor("t2b_all", [128, TC * 128], BF16))
            cntf = es4.enter_context(nc.sbuf_tensor("cntf", [1, 256], F32))
            cnt_row = es4.enter_context(nc.sbuf_tensor("cnt_row", [1, 16], F32))
            ps_tp4 = [es4.enter_context(nc.psum_tensor(f"ps_tp4{i}", [128, 1024], BF16))
                      for i in range(2)]
            ps_h = es4.enter_context(nc.psum_tensor("ps_h", [128, 512], F32))
            ps_tmp = es4.enter_context(nc.psum_tensor("ps_tmp", [128, 512], F32))
            ps_hT4 = es4.enter_context(nc.psum_tensor("ps_hT4", [32, TC * 128], BF16))
            ps_g = es4.enter_context(nc.psum_tensor("ps_g", [128, 128], F32))
            ps_cnt2 = es4.enter_context(nc.psum_tensor("ps_cnt2", [1, 256], F32))
            bbuf = [es4.enter_context(nc.sbuf_tensor(f"bb{i}", [128, D], BF16))
                    for i in range(2)]
            xbuf4 = [es4.enter_context(nc.sbuf_tensor(f"xb4{i}", [128, D], F32))
                     for i in range(2)]
            x2_sb = [es4.enter_context(nc.sbuf_tensor(f"x2_{i}", [128, D], F32))
                     for i in range(2)]
            l2bf = [es4.enter_context(nc.sbuf_tensor(f"l2b{i}", [128, D], BF16))
                    for i in range(2)]
            l2t = [es4.enter_context(nc.sbuf_tensor(f"l2t{i}", [128, D], BF16))
                   for i in range(2)]

            G.wait_ge(s_aev.h, s_aev.v)
            s_const.inc(G.dma_start(
                out=gw1_sb[:, :].rearrange("p (kc g) -> p kc g", kc=KC),
                in_=gw1T_d.rearrange("(kc kp) g -> kp kc g", kp=128)), 16)
            s_const.inc(G.dma_start(
                out=A_sb[:, :].rearrange("p (kc er) -> p kc er", kc=KC),
                in_=aRe_d.rearrange("(kc kp) er -> kp kc er", kp=128)), 16)

            SY.wait_ge(s_aev.h, s_aev.v)
            for i in range(8):
                SY.wait_ge(s_stg[i].h, s_stg[i].v)

            for tc in range(TC):
                V.bn_aggr(mvb[:, 2 * tc:2 * tc + 2], bstats[:, tc * 48:(tc + 1) * 48])
                V.drain()
            s_v4.inc(V.tensor_scalar(nmb[:, :],
                                     mvb[:, :].rearrange("p (t two) -> p t two", two=2)[:, :, 0:1].squeeze(2),
                                     -1.0, None, OP.mult))
            A.wait_ge(s_v4.h, s_v4.v)
            A.activation(rsb[:, :],
                         mvb[:, :].rearrange("p (t two) -> p t two", two=2)[:, :, 1:2].squeeze(2),
                         AF.Ln, bias=eps_t[:, 0:1], scale=1.0)
            A.drain()
            s_a4.inc(A.activation(rsb[:, :], rsb[:, :], AF.Exp, scale=-0.5))
            V.wait_ge(s_a4.h, s_a4.v)
            s_v4.inc(V.tensor_tensor(nmb[:, :], nmb[:, :], rsb[:, :], OP.mult))
            V.drain()
            es_bst.close()

            a4_gate_prev = 0
            ptc = 0
            stt_v4 = [0, 0]
            ln_a4 = [0, 0]
            for step in range(TC + 1):
                if step < TC:
                    tc = step
                    sl = tc % 2
                    if tc >= 1:
                        psl = (tc - 1) % 2
                        SQ4 = SY if psl == 0 else SC
                        SQ4.wait_ge(s_v4.h, stt_v4[psl])
                        s_x2s[psl].inc(SQ4.dma_start(
                            out=x2_d[(tc - 1) * 128:tc * 128, :],
                            in_=x2_sb[psl][:, :]), 16)
                    if tc >= 2:
                        SY.wait_ge(s_v4.h, stt_v4[sl])
                        G.wait_ge(s_v4.h, stt_v4[sl])
                    s_pb[sl].inc(SY.dma_start(
                        out=bbuf[sl][:, :],
                        in_=b_dram.ap()[tc * 128:(tc + 1) * 128, :]), 16)
                    s_px[sl].inc(G.dma_start(
                        out=xbuf4[sl][:, :], in_=x_d[tc * 128:(tc + 1) * 128, :]), 16)
                    V.wait_ge(s_pb[sl].h, s_pb[sl].v)
                    V.wait_ge(s_px[sl].h, s_px[sl].v)
                    if tc >= 2:
                        V.wait_ge(s_x2s[sl].h, s_x2s[sl].v)
                    V.tensor_scalar(xbuf4[sl][:, :], xbuf4[sl][:, :],
                                    nmb[:, tc:tc + 1], None, OP.add)
                    V.drain()
                    s_v4.inc(V.scalar_tensor_tensor(
                        x2_sb[sl][:, :], bbuf[sl][:, :], rsb[:, tc:tc + 1],
                        xbuf4[sl][:, :], OP.mult, OP.add))
                    stt_v4[sl] = s_v4.v
                    for i in range(8):
                        V.bn_stats(stats[sl][:, 6 * i:6 * i + 6],
                                   x2_sb[sl][:, 512 * i:512 * (i + 1)])
                    V.drain()
                    V.bn_aggr(mv[sl][:, :], stats[sl][:, 0:48])
                    V.drain()
                    s_v4.inc(V.tensor_scalar(nm[sl][:, :], mv[sl][:, 0:1], -1.0,
                                             None, OP.mult))
                    A.wait_ge(s_v4.h, s_v4.v)
                    A.activation(rstd[sl][:, :], mv[sl][:, 1:2], AF.Ln,
                                 bias=eps_t[:, 0:1], scale=1.0)
                    A.drain()
                    s_a4.inc(A.activation(rstd[sl][:, :], rstd[sl][:, :], AF.Exp,
                                          scale=-0.5))
                    ln_a4[sl] = s_a4.v
                if step < 1:
                    continue
                tc = step - 1
                sl = tc % 2
                V.wait_ge(s_a4.h, ln_a4[sl])
                if tc >= 2:
                    V.wait_ge(s_l2u.h, tc - 1)
                V.tensor_tensor(nmr[sl][:, :], nm[sl][:, :], rstd[sl][:, :], OP.mult)
                V.drain()
                s_v4.inc(V.tensor_scalar(l2bf[sl][:, :], x2_sb[sl][:, :],
                                         rstd[sl][:, 0:1], nmr[sl][:, 0:1],
                                         OP.mult, OP.add))

                PE.wait_ge(s_v4.h, s_v4.v)
                if tc >= 2:
                    PE.wait_ge(s_l2[sl].h, s_l2[sl].v)
                for kc in range(KC):
                    bank = (ptc // 8) % 2
                    slot = ptc % 8
                    if slot == 0 and ptc >= 16:
                        PE.wait_ge(s_tp4.h, ptc // 8 - 1)
                    s_pe4.inc(PE.transpose(
                        ps_tp4[bank][:, slot * 128:(slot + 1) * 128],
                        l2bf[sl][:, kc * 128:(kc + 1) * 128], ident[:, :]))
                    if kc == KC - 1:
                        s_l2u.inc(PE.nop())
                    if slot == 7:
                        A.wait_ge(s_pe4.h, s_pe4.v)
                        s_tp4.inc(A.copy(
                            l2t[sl][:, (kc - 7) * 128:(kc + 1) * 128],
                            ps_tp4[bank][:, 0:1024]))
                    ptc += 1
                SC.wait_ge(s_tp4.h, s_tp4.v)
                s_l2[sl].inc(SC.dma_start(
                    out=l2t_d[tc * 128:(tc + 1) * 128, :], in_=l2t[sl][:, :]), 16)

                PE.wait_ge(s_tp4.h, s_tp4.v)
                PE.wait_ge(s_a4.h, a4_gate_prev)
                PE.wait_ge(s_const.h, s_const.v)
                PE.wait_ge(s_cpool.h, s_cpool.v)
                mmi = None
                for kc in range(KC):
                    blkap = l2t[sl][:, kc * 128:(kc + 1) * 128]
                    PE.matmul(ps_h[:, 0:32], blkap, gw1_sb[:, kc * 32:(kc + 1) * 32],
                              start=(kc == 0), stop=(kc == KC - 1))
                    mmi = PE.matmul(ps_tmp[:, 0:128], blkap,
                                    A_sb[:, kc * 128:(kc + 1) * 128],
                                    start=(kc == 0), stop=(kc == KC - 1))
                s_pe4.inc(mmi)
                s_l2[sl].inc(PE.nop())
                A.wait_ge(s_pe4.h, s_pe4.v)
                A.copy(h_all[:, tc * 32:(tc + 1) * 32], ps_h[:, 0:32])
                A.drain()
                s_a4.inc(A.copy(tmpf_all[:, tc * 128:(tc + 1) * 128],
                                ps_tmp[:, 0:128]))
                a4_gate_prev = s_a4.v

            SC.wait_ge(s_v4.h, stt_v4[1])
            s_x2s[1].inc(SC.dma_start(
                out=x2_d[(TC - 1) * 128:TC * 128, :], in_=x2_sb[1][:, :]), 16)
            loopend = {s: s.v for s in (s_v4, s_a4, s_pe4,
                                        s_l2[0], s_l2[1], s_x2s[0], s_x2s[1])}

            hv = h_all[:, :].rearrange("p (c g) -> p c g", c=TC)
            hqv = hsq[:, :].rearrange("p (c g) -> p c g", c=TC)
            gb1v = gb1B[:, :].unsqueeze(1).broadcast_to([128, TC, 32])
            gamv = gamB[:, :].unsqueeze(1).broadcast_to([128, TC, 32])
            betv = betB[:, :].unsqueeze(1).broadcast_to([128, TC, 32])
            V.wait_ge(s_a4.h, s_a4.v)
            V.tensor_tensor(hv, hv, gb1v, OP.add)
            V.drain()
            V.tensor_reduce(hmu[:, :], hv, axis=AX, op=OP.add)
            V.drain()
            V.tensor_scalar(hmu[:, :], hmu[:, :], -1.0 / 32.0, None, OP.mult)
            V.drain()
            V.tensor_tensor(hv, hv,
                            hmu[:, :].unsqueeze(2).broadcast_to([128, TC, 32]),
                            OP.add)
            V.drain()
            V.tensor_tensor(hqv, hv, hv, OP.mult)
            V.drain()
            V.tensor_reduce(hvar[:, :], hqv, axis=AX, op=OP.add)
            V.drain()
            s_v4.inc(V.tensor_scalar(hvar[:, :], hvar[:, :], 1.0 / 32.0,
                                     None, OP.mult))
            A.wait_ge(s_v4.h, s_v4.v)
            A.activation(hrs[:, :], hvar[:, :], AF.Ln, bias=eps_t[:, 0:1], scale=1.0)
            A.drain()
            s_a4.inc(A.activation(hrs[:, :], hrs[:, :], AF.Exp, scale=-0.5))
            V.wait_ge(s_a4.h, s_a4.v)
            V.tensor_tensor(hv, hv,
                            hrs[:, :].unsqueeze(2).broadcast_to([128, TC, 32]),
                            OP.mult)
            V.drain()
            V.tensor_tensor(hv, hv, gamv, OP.mult)
            V.drain()
            V.tensor_tensor(hv, hv, betv, OP.add)
            V.drain()
            V.tensor_scalar(h_all[:, :], h_all[:, :], 0.0, None, OP.max)
            V.drain()
            s_v4.inc(V.tensor_copy(hb_all[:, :], h_all[:, :]))
            PE.wait_ge(s_v4.h, s_v4.v)
            for tc in range(TC):
                s_pe4.inc(PE.transpose(ps_hT4[:, tc * 128:(tc + 1) * 128],
                                       hb_all[:, tc * 32:(tc + 1) * 32],
                                       ident[:, :]))
            A.wait_ge(s_pe4.h, s_pe4.v)
            s_a4.inc(A.copy(hT_all[:, :], ps_hT4[:, :]))
            PE.wait_ge(s_a4.h, s_a4.v)
            mmi = None
            for tc in range(TC):
                mmi = PE.matmul(ps_g[:, tc * 8:(tc + 1) * 8],
                                hT_all[0:32, tc * 128:(tc + 1) * 128],
                                gw2_sb[0:32, :],
                                start=True, stop=True)
            s_pe4.inc(mmi)
            A.wait_ge(s_pe4.h, s_pe4.v)
            s_a4.inc(A.copy(gat_all[:, :], ps_g[:, 0:128]))
            gv = gat_all[:, :].rearrange("p (c e) -> p c e", c=TC)
            gv2 = gat2_all[:, :].rearrange("p (c e) -> p c e", c=TC)
            e1v = eq1a[:, :].rearrange("p (c e) -> p c e", c=TC)
            e2v = eq2a[:, :].rearrange("p (c e) -> p c e", c=TC)
            V.wait_ge(s_a4.h, s_a4.v)
            V.tensor_tensor(gv, gv,
                            gb2B[:, :].unsqueeze(1).broadcast_to([128, TC, 8]),
                            OP.add)
            V.drain()
            V.reduce_max(m1a[:, :], gv, axis=AX)
            V.drain()
            V.tensor_tensor(e1v, gv,
                            m1a[:, :].unsqueeze(2).broadcast_to([128, TC, 8]),
                            OP.is_equal)
            V.drain()
            V.scalar_tensor_tensor(gv2, e1v, BIG_NEG, gv, OP.mult, OP.add)
            V.drain()
            V.reduce_max(m2a[:, :], gv2, axis=AX)
            V.drain()
            V.tensor_tensor(e2v, gv2,
                            m2a[:, :].unsqueeze(2).broadcast_to([128, TC, 8]),
                            OP.is_equal)
            V.drain()
            ebv = eqb_all[:, :].rearrange("p (c k) -> p c k", c=TC)
            V.tensor_copy(ebv[:, :, 0:8], e1v)
            V.drain()
            s_v4.inc(V.tensor_copy(ebv[:, :, 8:16], e2v))
            PE.wait_ge(s_v4.h, s_v4.v)
            s_pe4.inc(PE.matmul(ps_cnt2[0:1, 0:256], ones_b[:, :], eqb_all[:, :],
                                start=True, stop=True))
            A.wait_ge(s_pe4.h, s_pe4.v)
            s_a4.inc(A.copy(cntf[:, :], ps_cnt2[0:1, 0:256]))
            V.wait_ge(s_a4.h, s_a4.v)
            s_v4.inc(V.tensor_reduce(
                cnt_row[:, :], cntf[:, :].rearrange("p (c k) -> p k c", c=TC),
                axis=AX, op=OP.add))
            cnt_v4 = s_v4.v
            G.wait_ge(s_v4.h, cnt_v4)
            s_cci.inc(G.dma_start(out=cc_in.ap(), in_=cnt_row[:, :]), 16)
            G.wait_ge(s_cci.h, s_cci.v)
            G.collective_compute(
                "AllReduce", OP.add, replica_groups=[list(range(NCORES))],
                ins=[cc_in.ap().opt()], outs=[cc_out.ap().opt()],
            ).then_inc(s_ccd)
            s_v4.inc(V.scalar_tensor_tensor(dga[:, :], m1a[:, :], -1.0, m2a[:, :],
                                            OP.mult, OP.add))
            A.wait_ge(s_v4.h, s_v4.v)
            s_a4.inc(A.activation(sxa[:, :], dga[:, :], AF.Exp))
            V.wait_ge(s_a4.h, s_a4.v)
            V.tensor_scalar(spa[:, :], sxa[:, :], 1.0, None, OP.add)
            V.drain()
            V.reciprocal(rsa[:, :], spa[:, :])
            V.drain()
            V.tensor_tensor(p2a[:, :], sxa[:, :], rsa[:, :], OP.mult)
            V.drain()
            V.tensor_scalar(p1a[:, :], p2a[:, :], -ALPHA, ALPHA, OP.mult, OP.add)
            V.drain()
            V.tensor_scalar(p2a[:, :], p2a[:, :], ALPHA, None, OP.mult)
            V.drain()
            V.tensor_tensor(g1a[:, :].rearrange("p (c e) -> p c e", c=TC), e1v,
                            p1a[:, :].unsqueeze(2).broadcast_to([128, TC, 8]),
                            OP.mult)
            V.drain()
            V.tensor_tensor(g2a[:, :].rearrange("p (c e) -> p c e", c=TC), e2v,
                            p2a[:, :].unsqueeze(2).broadcast_to([128, TC, 8]),
                            OP.mult)
            V.drain()
            tv = tmpf_all[:, :].rearrange("p (c ee r) -> p c ee r", c=TC, ee=8)
            V.tensor_tensor(
                t1b_all[:, :].rearrange("p (c ee r) -> p c ee r", c=TC, ee=8), tv,
                g1a[:, :].rearrange("p (c e) -> p c e", c=TC)
                .unsqueeze(3).broadcast_to([128, TC, 8, 16]), OP.mult)
            V.drain()
            s_v4.inc(V.tensor_tensor(
                t2b_all[:, :].rearrange("p (c ee r) -> p c ee r", c=TC, ee=8), tv,
                g2a[:, :].rearrange("p (c e) -> p c e", c=TC)
                .unsqueeze(3).broadcast_to([128, TC, 8, 16]), OP.mult))
            tp_base = s_tp4.v
            PE.wait_ge(s_v4.h, s_v4.v)
            PE.wait_ge(s_tp4.h, tp_base)
            tpi = 0
            for tsrc, tdst in ((t1b_all, t1_sb), (t2b_all, t2_sb)):
                for tc in range(TC):
                    bank = (tpi // 8) % 2
                    slot = tpi % 8
                    if slot == 0 and tpi >= 16:
                        PE.wait_ge(s_tp4.h, tp_base + tpi // 8 - 1)
                    s_pe4.inc(PE.transpose(
                        ps_tp4[bank][:, slot * 128:(slot + 1) * 128],
                        tsrc[:, tc * 128:(tc + 1) * 128], ident[:, :]))
                    if slot == 7:
                        A.wait_ge(s_pe4.h, s_pe4.v)
                        s_tp4.inc(A.copy(tdst[:, (tc - 7) * 128:(tc + 1) * 128],
                                         ps_tp4[bank][:, 0:1024]))
                    tpi += 1

        es_bflat = ExitStack()
        ebf = es_bflat.enter_context
        tot = ebf(nc.sbuf_tensor("tot", [1, 16], F32))
        alw = ebf(nc.sbuf_tensor("alw", [1, 16], F32))
        alw_rep = ebf(nc.sbuf_tensor("alw_rep", [1, 256], F32))
        a0x = ebf(nc.sbuf_tensor("a0x", [128, 1], F32))
        a1x = ebf(nc.sbuf_tensor("a1x", [128, 1], F32))
        estats = ebf(nc.sbuf_tensor("estats", [128, TC * NTN * 6], F32))
        tall = ebf(nc.sbuf_tensor("tall", [128, TOK], BF16))

        es_q = ExitStack()
        b_sb = es_q.enter_context(nc.sbuf_tensor("b_sb", [128, D], BF16))
        l2T = es_q.enter_context(nc.sbuf_tensor("l2T", [128, KC * TOK], BF16))

        l2Tv2 = l2T[:, :].rearrange("p (kc t) -> p kc t", kc=KC)
        for Q in (SY, SC):
            Q.wait_ge(s_v4.h, s_v4.v)
            Q.wait_ge(s_a4.h, s_a4.v)
            Q.wait_ge(s_pe4.h, s_pe4.v)
            Q.wait_ge(s_l2[0].h, s_l2[0].v)
            Q.wait_ge(s_l2[1].h, s_l2[1].v)
            Q.wait_ge(s_x2s[0].h, s_x2s[0].v)
            Q.wait_ge(s_x2s[1].h, s_x2s[1].v)

        with ExitStack() as es3:
            wf = [es3.enter_context(nc.sbuf_tensor(f"wf{i}", [128, KH * NTW], BF16))
                  for i in range(NWS)]
            ps_acc = [es3.enter_context(nc.psum_tensor(f"ps_accB{i}", [128, 512], F32))
                      for i in range(8)]

            def wf_load(sidx):
                kh = sidx % 2
                nt = sidx // 2
                wsl = sidx % NWS
                if sidx >= NWS:
                    G.wait_ge(s_wffB[wsl].h, s_wffB[wsl].v)
                s_wfB[wsl].inc(G.dma_start(
                    out=wf[wsl][:, :].rearrange("p (kc n) -> p kc n", kc=KH),
                    in_=wfT_d[kh * KH * 128:(kh + 1) * KH * 128,
                              nt * NTW:(nt + 1) * NTW]
                        .rearrange("(kc kp) n -> kp kc n", kp=128)), 16)

            G.wait_ge(s_mm.h, s_mm.v)
            G.wait_ge(s_aev.h, s_aev.v)
            for i in range(8):
                G.wait_ge(s_stg[i].h, s_stg[i].v)
            for s, v in loopend.items():
                G.wait_ge(s.h, v)
            wf_load(0)
            wf_load(1)
            s_q1.inc(G.dma_start(out=b_sb[:, :], in_=bRe_d), 16)
            wf_load(2)
            G.wait_ge(s_ccd, 1)
            s_cco.inc(G.dma_start(out=tot[:, :], in_=cc_out.ap()), 16)

            l2mark = [None] * TC
            for tc in range(TC):
                Q, sem = (SY, s_l2B) if tc % 2 == 0 else (SC, s_l2C)
                sem.inc(Q.dma_start(
                    out=l2Tv2[:, :, tc * 128:(tc + 1) * 128],
                    in_=l2t_d[tc * 128:(tc + 1) * 128, :]
                        .rearrange("p (kc t) -> p kc t", kc=KC)), 16)
                l2mark[tc] = (sem, sem.v)

            V.wait_ge(s_cco.h, s_cco.v)
            V.tensor_scalar(alw[:, :], tot[:, :], CAPACITY + 0.5, None, OP.is_lt)
            V.drain()
            s_v3B.inc(V.tensor_copy(
                alw_rep[:, :].rearrange("p (k e r) -> p k e r", k=2, e=8),
                alw[:, :].rearrange("p (k e) -> p k e", k=2)
                .unsqueeze(3).broadcast_to([1, 2, 8, 16])))
            G.wait_ge(s_v3B.h, s_v3B.v)
            s_awx.inc(G.dma_start(out=awd_dram.ap(), in_=alw_rep[:, :]), 16)
            G.wait_ge(s_awx.h, 16)
            awd_col = awd_dram.ap().rearrange("o (a b) -> (o a) b", b=1)
            s_awx.inc(G.dma_start(out=a0x[:, :], in_=awd_col[0:128, :]), 16)
            s_awx.inc(G.dma_start(out=a1x[:, :], in_=awd_col[128:256, :]), 16)
            V.wait_ge(s_awx.h, s_awx.v)
            V.wait_ge(s_tp4.h, s_tp4.v)
            V.tensor_scalar(tall[:, :], t2_sb[:, :], a1x[:, 0:1], None, OP.mult)
            V.drain()
            s_v3B.inc(V.scalar_tensor_tensor(tall[:, :], t1_sb[:, :], a0x[:, 0:1],
                                             tall[:, :], OP.mult, OP.add))
            es_t14.close()
            stage = [es3.enter_context(nc.sbuf_tensor(f"stgB{i}", [128, NTW], BF16))
                     for i in range(8)]

            grp = 0
            for nt in range(NTN):
                for kh in range(2):
                    sidx = nt * 2 + kh
                    if sidx >= 3:
                        wf_load(sidx)
                for tg in range(TG):
                    bank = (grp % 2) * 4
                    if nt == 0:
                        for tci in range(4):
                            sem, val = l2mark[tg * 4 + tci]
                            PE.wait_ge(sem.h, val)
                    if grp == 0:
                        PE.wait_ge(s_q1.h, s_q1.v)
                    if grp >= 2:
                        PE.wait_ge(s_aevB.h, 4 * (grp - 1))
                    for kc in range(KC):
                        wsl = (nt * 2 + kc // KH) % NWS
                        if kc % KH == 0:
                            PE.wait_ge(s_wfB[wsl].h, s_wfB[wsl].v)
                        for tci in range(4):
                            tc = tg * 4 + tci
                            PE.matmul(
                                ps_acc[bank + tci][:, 0:NTW],
                                l2T[:, kc * TOK + tc * 128:kc * TOK + (tc + 1) * 128],
                                wf[wsl][:, (kc % KH) * NTW:(kc % KH + 1) * NTW],
                                start=(kc == 0), stop=False)
                    if grp == 0:
                        PE.wait_ge(s_v3B.h, s_v3B.v)
                    mmi = None
                    for tci in range(4):
                        tc = tg * 4 + tci
                        mmi = PE.matmul(ps_acc[bank + tci][:, 0:NTW],
                                        tall[:, tc * 128:(tc + 1) * 128],
                                        b_sb[:, nt * NTW:(nt + 1) * NTW],
                                        start=False, stop=True)
                    s_mmB.inc(mmi)
                    if tg == TG - 1:
                        s_wffB[(nt * 2) % NWS].inc(PE.nop())
                        s_wffB[(nt * 2 + 1) % NWS].inc(PE.nop())
                    A.wait_ge(s_mmB.h, s_mmB.v)
                    for tci in range(4):
                        tc = tg * 4 + tci
                        ssl = (grp % 2) * 4 + tci
                        uses = grp // 2
                        if uses >= 1:
                            A.wait_ge(s_stgB[ssl].h, 17 * uses)
                        s_aevB.inc(A.copy(stage[ssl][:, :], ps_acc[bank + tci][:, 0:NTW]))
                        SC.wait_ge(s_aevB.h, s_aevB.v)
                        s_stgB[ssl].inc(SC.dma_start(
                            out=e_dram.ap()[tc * 128:(tc + 1) * 128,
                                            nt * NTW:(nt + 1) * NTW],
                            in_=stage[ssl][:, :]), 16)
                        V.wait_ge(s_aevB.h, s_aevB.v)
                        s_stgB[ssl].inc(V.bn_stats(
                            estats[:, (tc * NTN + nt) * 6:(tc * NTN + nt) * 6 + 6],
                            stage[ssl][:, :]))
                    grp += 1
        es_q.close()

        with ExitStack() as es4B:
            ebuf = [es4B.enter_context(nc.sbuf_tensor(f"eb{i}", [128, D], BF16))
                    for i in range(4)]
            x2b = [es4B.enter_context(nc.sbuf_tensor(f"x2b{i}", [128, D], F32))
                   for i in range(4)]
            o_sb = [es4B.enter_context(nc.sbuf_tensor(f"o_sb{i}", [128, D], F32))
                    for i in range(4)]
            mvbB = es4B.enter_context(nc.sbuf_tensor("mvbB", [128, 2 * TC], F32))
            rsbB = es4B.enter_context(nc.sbuf_tensor("rsbB", [128, TC], F32))
            nmbB = es4B.enter_context(nc.sbuf_tensor("nmbB", [128, TC], F32))

            for Q in (SY, SC, G):
                Q.wait_ge(s_aevB.h, s_aevB.v)
                for i in range(8):
                    Q.wait_ge(s_stgB[i].h, s_stgB[i].v)
            for tc in range(TC):
                V.bn_aggr(mvbB[:, 2 * tc:2 * tc + 2], estats[:, tc * 48:(tc + 1) * 48])
                V.drain()
            mview = mvbB[:, :].rearrange("p (t two) -> p t two", two=2)
            s_v3B.inc(V.tensor_scalar(nmbB[:, :], mview[:, :, 0:1].squeeze(2),
                                      -1.0, None, OP.mult))
            A.wait_ge(s_v3B.h, s_v3B.v)
            A.activation(rsbB[:, :], mview[:, :, 1:2].squeeze(2),
                         AF.Ln, bias=eps_t[:, 0:1], scale=1.0)
            A.drain()
            s_a3B.inc(A.activation(rsbB[:, :], rsbB[:, :], AF.Exp, scale=-0.5))
            V.wait_ge(s_a3B.h, s_a3B.v)
            s_v3B.inc(V.tensor_tensor(nmbB[:, :], nmbB[:, :], rsbB[:, :], OP.mult))
            V.drain()
            s_e = s_peB[0]
            s_xq = {SC: s_pxxB[0], G: s_pxxB[1], SY: s_p3B}
            s_oq = {G: s_outB, SC: s_peB[1]}
            cmp_mark = [0, 0, 0, 0]
            st_mark = [None] * 4
            for tc in range(TC):
                sl = tc % 4
                XQ = (SC, G, SY)[tc % 3]
                OQ = G if tc % 2 == 0 else SC
                if tc >= 4:
                    SY.wait_ge(s_v3B.h, cmp_mark[sl])
                s_e.inc(SY.dma_start(
                    out=ebuf[sl][:, :],
                    in_=e_dram.ap()[tc * 128:(tc + 1) * 128, :]), 16)
                if tc >= 4:
                    XQ.wait_ge(s_v3B.h, cmp_mark[sl])
                sx = s_xq[XQ]
                sx.inc(XQ.dma_start(
                    out=x2b[sl][:, :], in_=x2_d[tc * 128:(tc + 1) * 128, :]), 16)
                V.wait_ge(sx.h, sx.v)
                V.tensor_scalar(x2b[sl][:, :], x2b[sl][:, :],
                                nmbB[:, tc:tc + 1], None, OP.add)
                V.drain()
                V.wait_ge(s_e.h, s_e.v)
                if tc >= 4:
                    V.wait_ge(st_mark[sl][0].h, st_mark[sl][1])
                s_v3B.inc(V.scalar_tensor_tensor(
                    o_sb[sl][:, :], ebuf[sl][:, :], rsbB[:, tc:tc + 1],
                    x2b[sl][:, :], OP.mult, OP.add))
                cmp_mark[sl] = s_v3B.v
                OQ.wait_ge(s_v3B.h, cmp_mark[sl])
                so = s_oq[OQ]
                so.inc(OQ.dma_start(
                    out=out_d[tc * 128:(tc + 1) * 128, :],
                    in_=o_sb[sl][:, :]), 16)
                st_mark[sl] = (so, so.v)
            SC.wait_ge(s_oq[G].h, s_oq[G].v)
            SC.wait_ge(s_oq[SC].h, s_oq[SC].v)
        es_bflat.close()

    return nc



def kernel(L0_attn_data, L0_attn_weight, L0_attn_mask, L0_ffn_weight,
           L0_ffn_lora_A, L0_ffn_lora_B, L0_ffn_gw1, L0_ffn_gb1,
           L0_ffn_ln_gamma, L0_ffn_ln_beta, L0_ffn_gw2, L0_ffn_gb2):
    x = np.asarray(L0_attn_data, dtype=np.float32)
    waT = np.ascontiguousarray(np.asarray(L0_attn_weight, np.float32).T)
    wfT = np.ascontiguousarray(np.asarray(L0_ffn_weight, np.float32).T)
    mask = np.asarray(L0_attn_mask, np.int32)
    maskR = np.ascontiguousarray(
        mask.reshape(128, 32, 32, 4, 32).transpose(3, 2, 0, 4, 1).reshape(128, 128, 1024))
    aRe = np.ascontiguousarray(
        np.asarray(L0_ffn_lora_A, np.float32).transpose(1, 0, 2).reshape(D, 128))
    bRe = np.ascontiguousarray(np.asarray(L0_ffn_lora_B, np.float32).reshape(128, D))
    gw1T = np.ascontiguousarray(np.asarray(L0_ffn_gw1, np.float32).T)
    gw2T = np.ascontiguousarray(np.tile(np.asarray(L0_ffn_gw2, np.float32).T, (4, 1)))
    gb1B = np.ascontiguousarray(np.broadcast_to(
        np.asarray(L0_ffn_gb1, np.float32), (128, 32)))
    gamB = np.ascontiguousarray(np.broadcast_to(
        np.asarray(L0_ffn_ln_gamma, np.float32), (128, 32)))
    betB = np.ascontiguousarray(np.broadcast_to(
        np.asarray(L0_ffn_ln_beta, np.float32), (128, 32)))
    gb2B = np.ascontiguousarray(np.broadcast_to(
        np.asarray(L0_ffn_gb2, np.float32), (128, 8)))

    nc = build()
    in_maps = [{
        "x": np.ascontiguousarray(x[c]),
        "waT": waT, "maskR": np.ascontiguousarray(maskR[:, 16 * c:16 * (c + 1), :]),
        "aRe": aRe,
        "gw1T": gw1T, "gw2T": gw2T,
        "gb1B": gb1B, "gamB": gamB, "betB": betB, "gb2B": gb2B,
        "wfT": wfT, "bRe": bRe,
    } for c in range(NCORES)]
    res = run_bass_kernel_spmd(nc, in_maps, core_ids=list(range(NCORES)))

    out = np.stack([res.results[c]["out"] for c in range(NCORES)], axis=0)
    return np.ascontiguousarray(out, dtype=np.float32)
